# version 10
# baseline (speedup 1.0000x reference)
import sys
if "/opt/trn_rl_repo" not in sys.path:
    sys.path.insert(0, "/opt/trn_rl_repo")

from contextlib import ExitStack

import numpy as np

import concourse.bass as bass
import concourse.mybir as mybir
from concourse.bass_utils import run_bass_kernel_spmd

F32 = mybir.dt.float32
BF16 = mybir.dt.bfloat16
I32 = mybir.dt.int32
Alu = mybir.AluOpType
Act = mybir.ActivationFunctionType
AX = mybir.AxisListType.X

T = 2048
D = 1024
H = 16
HD = 64
INNER = 4096
NT = T // 128
NCH = D // 128
NST = 4
SW = 512
NG = 4
GW = 512
NIT = INNER // 512
NICH = INNER // 128
MAGIC = 12582912.0
LN_EPS = 1e-5

BUILD_STAGE = 4
P0_TILES = NT
P1_PARTS = ("k", "q", "gkt", "v", "wm")


def _seal(eng, sem):
    eng.drain()
    return sem.inc(eng.nop())


class Sem:
    def __init__(self, ctx, nc, name):
        self.h = ctx.enter_context(nc.semaphore(name))
        self.v = 0

    def inc(self, instr, n=1):
        instr.then_inc(self.h, n)
        self.v += n
        return self.v


class K:
    pass


def build_program(taps=()):
    nc = bass.Bass("TRN2", target_bir_lowering=False, debug=False,
                   detect_race_conditions=False)
    k = K()
    k.nc = nc
    k.taps = {}

    k.xd = nc.dram_tensor("x", [T, D], F32, kind="ExternalInput").ap()
    k.w_q = nc.dram_tensor("w_q", [D, D], F32, kind="ExternalInput").ap()
    k.w_k = nc.dram_tensor("w_k", [D, D], F32, kind="ExternalInput").ap()
    k.w_v = nc.dram_tensor("w_v", [D, D], F32, kind="ExternalInput").ap()
    k.w_o = nc.dram_tensor("w_o", [D, D], F32, kind="ExternalInput").ap()
    k.g_q = nc.dram_tensor("g_q", [HD, 1], F32, kind="ExternalInput").ap()
    k.g_k = nc.dram_tensor("g_k", [HD, 1], F32, kind="ExternalInput").ap()
    k.b_o = nc.dram_tensor("b_o", [1, D], F32, kind="ExternalInput").ap()
    k.gamma = nc.dram_tensor("gamma", [1, D], F32, kind="ExternalInput").ap()
    k.beta = nc.dram_tensor("beta", [1, D], F32, kind="ExternalInput").ap()
    k.wgT = nc.dram_tensor("wgT", [D, INNER], F32, kind="ExternalInput").ap()
    k.wuT = nc.dram_tensor("wuT", [D, INNER], F32, kind="ExternalInput").ap()
    k.wdT = nc.dram_tensor("wdT", [INNER, D], F32, kind="ExternalInput").ap()
    k.yd = nc.dram_tensor("y", [T, D], F32, kind="ExternalOutput").ap()
    k.wgq_d = nc.dram_tensor("wgq_d", [D, INNER], BF16, kind="Internal").ap()
    k.wuq_d = nc.dram_tensor("wuq_d", [D, INNER], BF16, kind="Internal").ap()
    k.wdq_d = nc.dram_tensor("wdq_d", [INNER, D], BF16, kind="Internal").ap()
    k.rowscr = nc.dram_tensor("rowscr", [1, 4096], F32, kind="Internal").ap()
    for name, shape, dt in taps:
        k.taps[name] = nc.dram_tensor("tap_" + name, shape, dt,
                                      kind="ExternalOutput").ap()

    ctx = ExitStack()
    e = ctx.enter_context
    with ctx:
        for s in ["ld", "so", "gd", "pv", "pa", "pt", "pg"]:
            setattr(k, s, Sem(ctx, nc, s))

        base = (nc.sbuf_base + 31) // 32 * 32
        top = nc.sbuf_top

        def at(name, shape, dt, off):
            return nc.alloc_sbuf_tensor_at(name, shape, dt, offset=base + off)

        k.xnT = at("xnT", [128, NCH * T], BF16, 0)
        k.xmT = at("xmT", [128, NCH * T], BF16, 32768)
        k.acc = at("acc", [128, NT * D], F32, 0)
        k.kT = at("kT", [128, NCH * T], BF16, 65536)
        k.qT = at("qT", [128, NCH * T], BF16, 98304)
        k.vtm = at("vtm", [128, NT * D], BF16, 131072)
        k.pbuf = at("pbuf", [128, GW * INNER // 128], F32, 65536)
        k.hT = at("hT", [128, NICH * GW], BF16, 131072)
        C = 160 * 1024
        Dof_wm = 184 * 1024 + 8192
        k.xb = [at(f"xb{i}", [128, D], F32, C + 4096 * i) for i in range(2)]
        k.xnb = at("xnb", [128, D], F32, C + 12288)
        k.xmb = at("xmb", [128, D], F32, C + 16384)
        _cb_off = [20480, 22528, 8192, 10240]
        k.cb16 = [at(f"cb16_{i}", [128, D], BF16, C + _cb_off[i])
                  for i in range(4)]
        DW = 2 * SW
        k.rvb16 = at("rvb16", [2, DW], BF16, C + 0)
        k.wsb = [at(f"wsb{i}", [128, D], BF16, C + 2048 + 2048 * i)
                 for i in range(2)]
        k.sqb = [at(f"sqb{i}", [128, DW], BF16, C + 6144 + 2048 * i)
                 for i in range(2)]
        k.rb_sb = at("rb_sb", [128, DW], F32, C + 10240)
        k.rvrow = at("rvrow", [2, DW], F32, C + 14336)
        k.gsb = at("gsb", [2, DW], F32, C + 18432)
        k.wmb = [at(f"wmb{i}", [128, SW], F32, Dof_wm + 2048 * i)
                 for i in range(2)]
        k.wvb = at("wvb", [128, NCH * D], BF16, 32768)
        k.modb = [at(f"modb{i}", [128, 2 * SW], BF16, C + 2048 * i)
                  for i in range(3)]
        k.gqb = [at(f"gqb{i}", [128, SW], F32, C + 6144 + 2048 * i)
                 for i in range(2)]
        k.fus = at("fus", [128, NCH * SW], BF16, C + 10240)
        k.e16 = at("e16", [16, NCH * 128], BF16, C + 18432)
        k.e16i = at("e16i", [16, 128], I32, C + 20480)
        k.xrel = [at(f"xrel{i}", [128, D], F32, C + 4096 * i) for i in range(2)]
        k.xqi = [at(f"xqi{i}", [128, SW], I32, C + 8192 + 2048 * i)
                 for i in range(2)]
        k.xqTg = at("xqTg", [128, NCH * GW], BF16, C + 12288)
        k.sgb = [at(f"sgb{i}", [128, SW], F32, C + 20480 + 2048 * i)
                 for i in range(2)]
        k.wfq = [at(f"wfq{i}", [128, SW], F32, C + 2048 * i)
                 for i in range(4)]
        k.xq16 = [at(f"xq16_{i}", [128, SW], BF16, C + 20480 + 1024 * i)
                  for i in range(2)]
        k.qa16 = [at(f"qa16_{i}", [128, SW], BF16, C + 22528 + 1024 * i)
                  for i in range(2)]
        k._pa_sg = [0, 0]
        Dof = 184 * 1024
        k.gk_all = at("gk_all", [16, T], F32, Dof)
        k.wob = at("wob", [128, NCH * D], BF16, Dof)
        k.wqt = [at(f"wqt{i}", [128, SW], BF16, Dof + 1024 * i)
                 for i in range(16)]
        E = 200 * 1024
        off = E

        def nxt(name, shape, dt):
            nonlocal off
            t_ = at(name, shape, dt, off)
            nbytes = int(np.prod(shape[1:])) * mybir.dt.size(dt)
            off += (nbytes + 31) // 32 * 32
            return t_

        k.mv1 = nxt("mv1", [128, 2 * NT], F32)
        k.rs1 = nxt("rs1", [128, NT], F32)
        k.sxc = nxt("sxc", [128, NT], F32)
        k.sxi = nxt("sxi", [128, NT], F32)
        k.scr = nxt("scr", [128, 16], F32)
        k.bns = nxt("bns", [128, 12], F32)
        k.ind1b = nxt("ind1b", [128, 2], BF16)
        k.indgq = nxt("indgq", [128, 2], BF16)
        k.indgk = nxt("indgk", [128, 2], BF16)
        k.e2f = nxt("e2f", [2, 128], F32)
        k.e2b = nxt("e2b", [2, 128], BF16)
        k.onesf = nxt("onesf", [128, 1], F32)
        k.onesbf = nxt("onesbf", [128, 1], BF16)
        k.onesr = nxt("onesr", [1, 128], F32)
        k.idnb = nxt("idnb", [128, 128], BF16)
        k.idnf = nxt("idnf", [128, 128], F32)
        k.cvals = nxt("cvals", [128, 1], F32)
        nc.const_aps.register(F32, -8.1, k.cvals[:, 0:1])
        k.gq_all = nxt("gq_all", [16, T], BF16)
        k.gk_tok = nxt("gk_tok", [128, NT * 16], BF16)
        k.cg_c = nxt("cg_c", [128, 1], F32)
        k.cu_c = nxt("cu_c", [128, 1], F32)
        k.cd_c = nxt("cd_c", [128, 1], F32)
        k.thr_g = nxt("thr_g", [128, 1], F32)
        k.thr_u = nxt("thr_u", [128, 1], F32)
        k.thr_d = nxt("thr_d", [128, 1], F32)
        k.cgsx = nxt("cgsx", [128, NT], F32)
        k.kfac = nxt("kfac", [128, NT], F32)
        k.wmac = nxt("wmac", [128, 4], F32)
        k.rowv = nxt("rowv", [1, 12], F32)
        k.frep = nxt("frep", [128, 12], F32)
        k.rs1sx = nxt("rs1sx", [128, NT], F32)
        k.cdsh = nxt("cdsh", [128, 4], F32)
        k.wsum = nxt("wsum", [1, 4], F32)
        k.io32 = at("io32", [128, 128], I32, C + 0)
        k.bo_b = at("bo_b", [128, D], BF16, C + 22528)
        k.gam_b = at("gam_b", [128, D], F32, 65536)
        k.bet_b = at("bet_b", [128, D], F32, 65536 + 4096)
        assert off <= top - base, (off, top - base)

        pat = lambda name, shape, dt, bank: nc.alloc_psum_tensor(
            name, shape, dt)
        psum_base0 = nc.psum_base
        k.ps_mm = [pat(f"ps_mm{i}", [128, 512], F32, i) for i in range(4)]
        psum_base = nc.psum_base
        nc.psum_base = psum_base0
        k.ps_kq = [pat(f"ps_kq{i}", [128, 1024], F32, 2 * i)
                   for i in range(2)]
        nc.psum_base = psum_base
        k.ps_tp = [pat(f"ps_tp{i}", [128, 1024], BF16, 4 + i)
                   for i in range(2)]
        k.ps_ss = [pat(f"ps_ss{i}", [2, 512], F32, 6 + i) for i in range(2)]
        nc.psum_base = psum_base
        k.ps_sc = [pat(f"ps_sc{i}", [128, 1024], F32, 4 + 2 * i)
                   for i in range(2)]
        nc.psum_base = psum_base
        k.ps_tf = [pat(f"ps_tf{i}", [128, 512], F32, 4 + i) for i in range(2)]
        nc.psum_base = psum_base + 4

        with nc.Block() as blk:
            k.blk = blk
            emit_setup(k)
            if BUILD_STAGE >= 1:
                emit_p0(k)
            if BUILD_STAGE >= 2:
                emit_p1(k)
            if BUILD_STAGE >= 3:
                emit_p2(k)
            if BUILD_STAGE >= 4:
                emit_p3(k)
            emit_taps(k)

    return nc, k


def emit_setup(k):
    nc, blk = k.nc, k.blk
    ld, pv, pg = k.ld, k.pv, k.pg

    @blk.sync
    def _(sy):
        sy.dma_start(out=k.gam_b[:].unsqueeze(1),
                     in_=k.gamma.partition_broadcast(128)).then_inc(ld.h, 16)
        sy.dma_start(out=k.bet_b[:].unsqueeze(1),
                     in_=k.beta.partition_broadcast(128)).then_inc(ld.h, 16)
        ld.v += 2 * 16
    k.LD_SETUP = ld.v

    @blk.gpsimd
    def _(gp):
        gd = k.gd
        pg.inc(gp.iota(k.io32[:], [[-1, 128]], base=0, channel_multiplier=1))
        gp.dma_start(out=k.indgq[0:64, 0:1], in_=k.g_q).then_inc(gd.h, 16)
        gp.dma_start(out=k.indgq[64:128, 1:2], in_=k.g_q).then_inc(gd.h, 16)
        gp.dma_start(out=k.indgk[0:64, 0:1], in_=k.g_k).then_inc(gd.h, 16)
        gp.dma_start(out=k.indgk[64:128, 1:2], in_=k.g_k).then_inc(gd.h, 16)
        gd.v += 4 * 16
        gp.wait_ge(k.pv.h, 1)
        pg.inc(gp.iota(k.io32[0:2, :], [[1, 2], [0, 64]], base=0,
                       channel_multiplier=-1))
    k.PG_IOTA = pg.v - 1
    k.PG_E2 = pg.v
    k.GD_SETUP = k.gd.v

    @blk.vector
    def _(v):
        v.wait_ge(ld.h, k.LD_SETUP)
        v.wait_ge(k.gd.h, k.GD_SETUP)
        v.memset(k.cvals[:, 0:1], -8.1)
        v.memset(k.ind1b[:], 0.0)
        v.memset(k.indgq[0:64, 1:2], 0.0)
        v.memset(k.indgq[64:128, 0:1], 0.0)
        v.memset(k.indgk[0:64, 1:2], 0.0)
        v.memset(k.indgk[64:128, 0:1], 0.0)
        v.memset(k.ind1b[0:64, 0:1], 1.0)
        v.memset(k.ind1b[64:128, 1:2], 1.0)
        v.memset(k.onesf[:], 1.0)
        v.memset(k.onesbf[:], 1.0)
        v.memset(k.onesr[:], 1.0)
        v.memset(k.wmac[:], 0.0)
        v.memset(k.wsum[:], 0.0)
        v.memset(k.rowv[:], 0.0)
        v.wait_ge(pg.h, k.PG_IOTA)
        v.tensor_scalar(out=k.idnb[:], in0=k.io32[:], scalar1=0.0,
                        scalar2=None, op0=Alu.is_equal)
        pv.inc(v.tensor_scalar(out=k.idnf[:], in0=k.io32[:], scalar1=0.0,
                               scalar2=None, op0=Alu.is_equal))
        v.wait_ge(pg.h, k.PG_E2)
        v.tensor_scalar(out=k.e2f[:], in0=k.io32[0:2, :], scalar1=0.0,
                        scalar2=None, op0=Alu.is_equal)
        pv.inc(v.tensor_scalar(out=k.e2b[:], in0=k.io32[0:2, :], scalar1=0.0,
                               scalar2=None, op0=Alu.is_equal))
    k.PV_CONST = pv.v


def emit_p0(k):
    nc, blk = k.nc, k.blk
    ld, pv, pa, pt = k.ld, k.pv, k.pa, k.pt
    PV_TILE = [0] * NT
    PV_EV = [0] * NT
    LD_TP = [0] * NT

    for t in range(P0_TILES):

        @blk.sync
        def _(sy, t=t):
            if t < 2:
                sy.wait_ge(pv.h, k.PV_CONST)
            else:
                sy.wait_ge(pv.h, PV_TILE[t - 2])
            sy.dma_start(out=k.xb[t % 2][:],
                         in_=k.xd[t * 128:(t + 1) * 128, :]).then_inc(ld.h, 16)
            ld.v += 16
        LD_T = ld.v

        @blk.vector
        def _(v, t=t):
            xt = k.xb[t % 2]
            if t == 0:
                v.wait_ge(pv.h, k.PV_CONST)
            v.wait_ge(ld.h, LD_T)
            v.bn_stats(out=k.bns[:, 0:6], in_=xt[:, 0:512])
            v.bn_stats(out=k.bns[:, 6:12], in_=xt[:, 512:1024])
            v.drain()
            v.bn_aggr(out=k.mv1[:, 2 * t:2 * t + 2], in_=k.bns[:])
            v.drain()
            v.tensor_scalar(
                out=k.scr[:, 0:1], in0=k.mv1[:, 2 * t + 1:2 * t + 2],
                scalar1=LN_EPS, scalar2=None, op0=Alu.add)
            v.drain()
            pv.inc(v.reciprocal(out=k.scr[:, 1:2], in_=k.scr[:, 0:1]))
        PV_R1 = pv.v

        @blk.scalar
        def _(s, t=t):
            s.wait_ge(pv.h, PV_R1)
            pa.inc(s.activation(out=k.rs1[:, t:t + 1], in_=k.scr[:, 1:2],
                                func=Act.Sqrt))
        PA_R1 = pa.v

        @blk.vector
        def _(v, t=t):
            xt = k.xb[t % 2]
            v.wait_ge(pa.h, PA_R1)
            PV_TILE[t] = pv.inc(v.tensor_scalar(
                out=k.xnb[:], in0=xt[:],
                scalar1=k.mv1[:, 2 * t:2 * t + 1],
                scalar2=k.rs1[:, t:t + 1],
                op0=Alu.subtract, op1=Alu.mult))
            v.wait_ge(pv.h, PV_TILE[t])
            v.reduce_max(out=k.scr[:, 2:3], in_=k.xnb[:], axis=AX,
                         apply_absolute_value=True)
            v.drain()
            v.tensor_scalar(out=k.scr[:, 2:3], in0=k.scr[:, 2:3],
                            scalar1=1e-5, scalar2=None, op0=Alu.max)
            v.drain()
            v.tensor_scalar(out=k.sxi[:, t:t + 1], in0=k.scr[:, 2:3],
                            scalar1=1.0 / 127.0, scalar2=None, op0=Alu.mult)
            v.reciprocal(out=k.scr[:, 3:4], in_=k.scr[:, 2:3])
            v.drain()
            v.tensor_scalar(out=k.sxc[:, t:t + 1], in0=k.scr[:, 3:4],
                            scalar1=127.0, scalar2=None, op0=Alu.mult)
            v.bn_stats(out=k.bns[:, 0:6], in_=k.xnb[:, 0:512])
            v.bn_stats(out=k.bns[:, 6:12], in_=k.xnb[:, 512:1024])
            v.drain()
            v.bn_aggr(out=k.scr[:, 6:8], in_=k.bns[:])
            v.drain()
            v.tensor_scalar(out=k.scr[:, 4:5], in0=k.scr[:, 7:8],
                            scalar1=LN_EPS, scalar2=None, op0=Alu.add)
            v.drain()
            pv.inc(v.reciprocal(out=k.scr[:, 5:6], in_=k.scr[:, 4:5]))
        PV_R2 = pv.v

        @blk.scalar
        def _(s, t=t):
            s.wait_ge(pv.h, PV_R2)
            pa.inc(s.activation(out=k.scr[:, 8:9], in_=k.scr[:, 5:6],
                                func=Act.Sqrt))
        PA_R2 = pa.v

        @blk.vector
        def _(v, t=t):
            v.wait_ge(pa.h, PA_R2)
            if t >= 2:
                v.wait_ge(ld.h, LD_TP[t - 2])
            v.tensor_scalar(out=k.xmb[:], in0=k.xnb[:],
                            scalar1=k.scr[:, 6:7],
                            scalar2=k.scr[:, 8:9],
                            op0=Alu.subtract, op1=Alu.mult)
            v.drain()
            v.tensor_tensor(out=k.xmb[:], in0=k.xmb[:], in1=k.gam_b[:],
                            op=Alu.mult)
            v.drain()
            v.tensor_tensor(out=k.xmb[:], in0=k.xmb[:], in1=k.bet_b[:],
                            op=Alu.add)
            v.tensor_copy(out=k.cb16[2 * (t % 2)][:], in_=k.xnb[:])
            v.drain()
            PV_TILE[t] = pv.inc(
                v.tensor_copy(out=k.cb16[2 * (t % 2) + 1][:], in_=k.xmb[:]))
        PV_CAST = pv.v

        @blk.sync
        def _(sy, t=t):
            sy.wait_ge(pv.h, PV_CAST)
            for c in range(NCH):
                sy.dma_start_transpose(
                    k.xnT[:, c * T + t * 128:c * T + (t + 1) * 128],
                    k.cb16[2 * (t % 2)][:, c * 128:(c + 1) * 128]).then_inc(
                        ld.h, 16)
                sy.dma_start_transpose(
                    k.xmT[:, c * T + t * 128:c * T + (t + 1) * 128],
                    k.cb16[2 * (t % 2) + 1][:, c * 128:(c + 1) * 128]
                ).then_inc(ld.h, 16)
            ld.v += 2 * NCH * 16
            LD_TP[t] = ld.v

    k.PV_P0_DONE = pv.v
    k.PT_P0_DONE = pt.v
    k.LD_P0_TP = LD_TP[P0_TILES - 1] if P0_TILES else ld.v


def emit_p1(k):
    nc, blk = k.nc, k.blk
    ld, gd, pv, pa, pt, pg = k.ld, k.gd, k.pv, k.pa, k.pt, k.pg
    PV_P0 = pv.v

    DW = 2 * SW
    NSP = NST // 2
    k._pvev = [0, 0]
    k._gd_gate = 0
    k._pa_g = 0
    k._pa_rv = 0
    k._pt_rb = 0
    k._pvrb = 0
    k._ptw = [0, 0]
    k._it = 0
    ps_rb = k.ps_sc[0]
    ps_ssw = k.ps_sc[1]

    def build_kq(which):
        wsrc = k.w_k if which == "k" else k.w_q
        dstT = k.kT if which == "k" else k.qT
        indg = k.indgk if which == "k" else k.indgq
        gall = k.gk_all if which == "k" else k.gq_all
        wviewed = wsrc.rearrange("(c p) n -> p c n", c=NCH)
        GD_W = [0, 0]
        for c2 in range(NCH):

            @blk.gpsimd
            def _(gp, c2=c2):
                if c2 == 0 and which == "k":
                    gp.wait_ge(pv.h, PV_P0)
                if k._ptw[c2 % 2]:
                    gp.wait_ge(pt.h, k._ptw[c2 % 2])
                gp.dma_start(
                    out=k.wsb[c2 % 2][:].rearrange("p (c n) -> p c n", c=NCH),
                    in_=wviewed[:, :, c2 * 128:(c2 + 1) * 128],
                ).then_inc(gd.h, 16)
                gd.v += 16
                GD_W[c2 % 2] = gd.v

            for sp in range(NSP):
                ping = k._it % 2
                k._it += 1
                ps_k = k.ps_kq[ping]
                sq = k.sqb[ping]

                @blk.tensor
                def _(tn, c2=c2, sp=sp, ping=ping, ps_k=ps_k):
                    if c2 == 0 and sp == 0 and which == "k":
                        tn.wait_ge(pv.h, PV_P0)
                        tn.wait_ge(ld.h, k.LD_P0_TP)
                    tn.wait_ge(gd.h, GD_W[c2 % 2])
                    if k._pvev[ping]:
                        tn.wait_ge(pv.h, k._pvev[ping])
                    wb = k.wsb[c2 % 2]
                    for half in range(2):
                        for ci in range(NCH):
                            mm = tn.matmul(
                                ps_k[:, half * SW:(half + 1) * SW],
                                wb[:, ci * 128:(ci + 1) * 128],
                                k.xmT[:, ci * T + sp * DW + half * SW:
                                      ci * T + sp * DW + (half + 1) * SW],
                                start=(ci == 0), stop=(ci == NCH - 1))
                    pt.inc(mm)
                PT_K = pt.v

                @blk.scalar
                def _(sc, ps_k=ps_k, sq=sq):
                    sc.wait_ge(pt.h, PT_K)
                    pa.inc(sc.activation(out=sq[:], in_=ps_k[:],
                                         func=Act.Square))
                PA_SQ = pa.v

                @blk.tensor
                def _(tn, sq=sq):
                    tn.wait_ge(pa.h, PA_SQ)
                    if k._pa_g:
                        tn.wait_ge(pa.h, k._pa_g)
                    for half in range(2):
                        mm = tn.matmul(
                            ps_ssw[0:2, half * SW:(half + 1) * SW],
                            k.ind1b[:], sq[:, half * SW:(half + 1) * SW],
                            start=True, stop=True)
                    pt.inc(mm)
                PT_SS = pt.v

                @blk.vector
                def _(v):
                    v.wait_ge(pt.h, PT_SS)
                    if k._pa_rv:
                        v.wait_ge(pa.h, k._pa_rv)
                    pv.inc(v.tensor_scalar(out=k.rvrow[:],
                                           in0=ps_ssw[0:2, :],
                                           scalar1=1e-24, scalar2=None,
                                           op0=Alu.max))
                PV_SG = pv.v

                @blk.scalar
                def _(sc):
                    sc.wait_ge(pv.h, PV_SG)
                    sc.activation(out=k.rvrow[:], in_=k.rvrow[:], func=Act.Ln)
                    sc.drain()
                    k._pa_rv = pa.inc(sc.activation(
                        out=k.rvrow[:], in_=k.rvrow[:], func=Act.Exp,
                        scale=-0.5))
                PA_RV = pa.v

                @blk.vector
                def _(v):
                    v.wait_ge(pa.h, PA_RV)
                    if k._pt_rb:
                        v.wait_ge(pt.h, k._pt_rb)
                    pv.inc(v.tensor_scalar(out=k.rvb16[:], in0=k.rvrow[:],
                                           scalar1=1e12, scalar2=None,
                                           op0=Alu.min))
                PV_RV = pv.v

                @blk.tensor
                def _(tn):
                    tn.wait_ge(pv.h, PV_RV)
                    if k._pvrb:
                        tn.wait_ge(pv.h, k._pvrb)
                    for half in range(2):
                        mm = tn.matmul(
                            ps_rb[:, half * SW:(half + 1) * SW], k.e2b[:],
                            k.rvb16[0:2, half * SW:(half + 1) * SW],
                            start=True, stop=True)
                    k._pt_rb = pt.inc(mm)
                PT_RB = pt.v

                @blk.vector
                def _(v, c2=c2, sp=sp, ping=ping, ps_k=ps_k):
                    v.wait_ge(pt.h, PT_RB)
                    k._pvrb = pv.inc(
                        v.tensor_copy(out=k.rb_sb[:], in_=ps_rb[:]))
                    v.wait_ge(pv.h, k._pvrb)
                    k._pvev[ping] = pv.inc(v.tensor_tensor(
                        out=dstT[:, c2 * T + sp * DW:c2 * T + (sp + 1) * DW],
                        in0=ps_k[:], in1=k.rb_sb[:], op=Alu.mult))
                PV_KT = pv.v

                @blk.tensor
                def _(tn, c2=c2, sp=sp):
                    tn.wait_ge(pv.h, PV_KT)
                    for half in range(2):
                        mm = tn.matmul(
                            ps_ssw[0:2, half * SW:(half + 1) * SW], indg[:],
                            dstT[:, c2 * T + sp * DW + half * SW:
                                 c2 * T + sp * DW + (half + 1) * SW],
                            start=True, stop=True)
                    pt.inc(mm)
                PT_DOT = pt.v

                @blk.scalar
                def _(sc):
                    sc.wait_ge(pt.h, PT_DOT)
                    if k._gd_gate:
                        sc.wait_ge(gd.h, k._gd_gate)
                    k._pa_g = pa.inc(sc.activation(out=k.gsb[:],
                                                   in_=ps_ssw[0:2, :],
                                                   func=Act.Sigmoid))
                PA_G = pa.v

                @blk.vector
                def _(v):
                    v.wait_ge(pa.h, PA_G)
                    pv.inc(v.tensor_scalar(
                        out=k.gsb[:], in0=k.gsb[:], scalar1=0.75,
                        scalar2=0.25, op0=Alu.mult, op1=Alu.add))
                PV_GG = pv.v

                @blk.gpsimd
                def _(gp, c2=c2, sp=sp):
                    gp.wait_ge(pv.h, PV_GG)
                    gp.dma_start(
                        out=gall[2 * c2:2 * c2 + 2, sp * DW:(sp + 1) * DW],
                        in_=k.gsb[:]).then_inc(gd.h, 16)
                    gd.v += 16
                    k._gd_gate = gd.v
                for _i3 in range(3):
                    emit_wm_piece()
            k._ptw[c2 % 2] = pt.v

    wm_pieces = []
    for mi, wm in enumerate([k.wgT, k.wuT]):
        wr = wm.rearrange("(c p) n -> c p n", p=128)
        for c in range(NCH):
            for j in range(8):
                wm_pieces.append((mi, wr[c, :, j * 512:(j + 1) * 512]))
    wdr = k.wdT.rearrange("(c p) n -> c p n", p=128)
    for c in range(NICH):
        for j in range(2):
            wm_pieces.append((2, wdr[c, :, j * 512:(j + 1) * 512]))
    LDW_P = [0] * len(wm_pieces)
    PV_WM = [0] * len(wm_pieces)
    k._wm_next = 0

    def emit_wm_piece():
        i = k._wm_next
        if i >= len(wm_pieces) or "wm" not in P1_PARTS:
            return
        k._wm_next += 1
        mi, ap = wm_pieces[i]

        @blk.sync
        def _(sy, i=i, ap=ap):
            if i >= 2:
                sy.wait_ge(pv.h, PV_WM[i - 2])
            sy.dma_start(out=k.wmb[i % 2][:], in_=ap).then_inc(ld.h, 16)
            ld.v += 16
            LDW_P[i] = ld.v

        @blk.vector
        def _(v, i=i, mi=mi):
            v.wait_ge(ld.h, LDW_P[i])
            v.reduce_sum(out=k.scr[:, 10:11], in_=k.wmb[i % 2][:], axis=AX,
                         apply_absolute_value=True)
            v.drain()
            PV_WM[i] = pv.inc(v.tensor_tensor(
                out=k.wmac[:, mi:mi + 1], in0=k.wmac[:, mi:mi + 1],
                in1=k.scr[:, 10:11], op=Alu.add))

    if "k" in P1_PARTS:
        build_kq("k")
    if "q" in P1_PARTS:
        build_kq("q")
    k.PV_KQ = k.pv.v

    if "gkt" not in P1_PARTS:
        k.PV_P1 = pv.v
        k.PT_P1 = pt.v
        k.PV_WSUM = pv.v
        return
    @blk.tensor
    def _(tn):
        tn.wait_ge(pv.h, k.PV_KQ)
        tn.wait_ge(gd.h, gd.v)
        for t in range(NT):
            tn.transpose(k.ps_mm[2][:, t * 16:(t + 1) * 16],
                         k.gk_all[:, t * 128:(t + 1) * 128],
                         k.idnf[0:16, 0:16])
            if t % 4 == 3:
                _seal(tn, pt)
                tn.wait_ge(pt.h, pt.v)
        _seal(tn, pt)
    PT_GKT = pt.v

    @blk.vector
    def _(v):
        v.wait_ge(pt.h, PT_GKT)
        pv.inc(v.tensor_copy(out=k.gk_tok[:], in_=k.ps_mm[2][:, 0:NT * 16]))
    PV_GKT = pv.v

    if "v" not in P1_PARTS:
        k.PV_P1 = pv.v
        k.PT_P1 = pt.v
        k.PV_WSUM = pv.v
        return
    @blk.gpsimd
    def _(gp):
        gp.wait_ge(pv.h, k.PV_KQ)
        wvr = k.w_v.rearrange("(c p) n -> c p n", p=128)
        for c in range(NCH):
            gp.dma_start(out=k.wvb[:, c * D:(c + 1) * D],
                         in_=wvr[c]).then_inc(gd.h, 16)
        gd.v += NCH * 16
    GD_WV = gd.v

    PV_VE = [0, 0, 0, 0]
    for t in range(NT):

        @blk.tensor
        def _(tn, t=t):
            if t == 0:
                tn.wait_ge(gd.h, GD_WV)
                tn.wait_ge(pv.h, PV_GKT)
            for half in range(2):
                ps_v = k.ps_mm[(2 * t + half) % 4]
                if PV_VE[(2 * t + half) % 4]:
                    tn.wait_ge(pv.h, PV_VE[(2 * t + half) % 4])
                for ci in range(NCH):
                    mm = tn.matmul(
                        ps_v[:],
                        k.xnT[:, ci * T + t * 128:ci * T + (t + 1) * 128],
                        k.wvb[:, ci * D + half * SW:ci * D + half * SW + SW],
                        start=(ci == 0), stop=(ci == NCH - 1))
                pt.inc(mm)
        PT_V = pt.v

        @blk.vector
        def _(v, t=t):
            for half in range(2):
                ps_v = k.ps_mm[(2 * t + half) % 4]
                v.wait_ge(pt.h, PT_V - 1 + half)
                gsl = k.gk_tok[:, t * 16 + half * 8:t * 16 + (half + 1) * 8]
                PV_VE[(2 * t + half) % 4] = pv.inc(v.tensor_tensor(
                    out=k.vtm[:, t * D + half * SW:t * D + (half + 1) * SW]
                    .rearrange("p (h e) -> p h e", e=HD),
                    in0=ps_v[:].rearrange("p (h e) -> p h e", e=HD),
                    in1=gsl.unsqueeze(2).broadcast_to([128, 8, HD]),
                    op=Alu.mult))

    k.PV_P1 = pv.v
    k.PT_P1 = pt.v

    if "wm" not in P1_PARTS:
        k.PV_WSUM = pv.v
        return
    while k._wm_next < len(wm_pieces) and "wm" in P1_PARTS:
        emit_wm_piece()

    @blk.tensor
    def _(tn):
        tn.wait_ge(pv.h, PV_WM[-1])
        tn.wait_ge(pv.h, k.PV_P1)
        for mi in range(3):
            mm = tn.matmul(k.ps_ss[0][0:1, mi:mi + 1], k.wmac[:, mi:mi + 1],
                           k.onesf[:], start=True, stop=True)
        pt.inc(mm)
    PT_WSUM = pt.v

    @blk.vector
    def _(v):
        v.wait_ge(pt.h, PT_WSUM)
        pv.inc(v.tensor_copy(out=k.wsum[0:1, 0:3], in_=k.ps_ss[0][0:1, 0:3]))
    k.PV_WSUM = pv.v


def emit_p2(k):
    nc, blk = k.nc, k.blk
    ld, gd, pv, pa, pt, pg = k.ld, k.gd, k.pv, k.pa, k.pt, k.pg
    PV_P1 = k.PV_P1
    GD_P1 = gd.v

    @blk.gpsimd
    def _(gp):
        gp.wait_ge(pt.h, k.PT_P1)
        gp.wait_ge(pv.h, k.PV_WSUM)
        wor = k.w_o.rearrange("(c p) n -> c p n", p=128)
        for c in range(NCH):
            gp.dma_start(out=k.wob[:, c * D:(c + 1) * D],
                         in_=wor[c]).then_inc(gd.h, 16)
        gd.v += NCH * 16
    k.GD_WO = gd.v

    PV_E16 = [0] * NCH
    for c2 in range(NCH):

        @blk.gpsimd
        def _(gp, c2=c2):
            if c2 == 0:
                gp.wait_ge(gd.h, GD_P1)
                gp.wait_ge(pv.h, k.PV_WSUM)
            else:
                gp.wait_ge(pv.h, PV_E16[c2 - 1])
            pg.inc(gp.iota(k.e16i[:], [[1, 2], [0, 64]], base=2 * c2,
                           channel_multiplier=-1))
        PG_E = pg.v

        @blk.vector
        def _(v, c2=c2):
            v.wait_ge(pg.h, PG_E)
            PV_E16[c2] = pv.inc(v.tensor_scalar(
                out=k.e16[:, c2 * 128:(c2 + 1) * 128],
                in0=k.e16i[:], scalar1=0.0, scalar2=None, op0=Alu.is_equal))

    @blk.gpsimd
    def _(gp):
        gp.wait_ge(pv.h, k.PV_WSUM)
        gp.dma_start(out=k.bo_b[:].unsqueeze(1),
                     in_=k.b_o.partition_broadcast(128)).then_inc(gd.h, 16)
        gd.v += 16
    GD_BO = gd.v

    pa_mod = [0, 0, 0]
    pt_fus = [0, 0, 0]
    pa_sc = [0, 0]
    pv_gqb = [0, 0]
    pv_fev = [0, 0]
    pv_oev = [0, 0]
    PV_ACC = [0] * NT
    pending_wo = []

    def fus_ap(s, c2, lo, hi):
        if s % 2 == 0:
            return k.fus[lo:hi, c2 * SW:(c2 + 1) * SW]
        return k.qT[lo:hi, c2 * T + (s - 1) * SW:c2 * T + s * SW]

    for s in range(NST):
        for c2 in range(NCH):
            jf = c2 % 2

            @blk.tensor
            def _(tn, s=s, c2=c2):
                if s == 0 and c2 == 0:
                    tn.wait_ge(pv.h, PV_P1)
                    tn.wait_ge(pv.h, PV_E16[-1])
                if pv_gqb[0]:
                    tn.wait_ge(pv.h, pv_gqb[0])
                pt.inc(tn.matmul(k.ps_mm[2][:],
                                 k.e16[:, c2 * 128:(c2 + 1) * 128],
                                 k.gq_all[:, s * SW:(s + 1) * SW],
                                 start=True, stop=True))
            PT_GQ = pt.v

            @blk.vector
            def _(v, s=s, c2=c2):
                v.wait_ge(pt.h, PT_GQ)
                pv_gqb[0] = pv.inc(
                    v.tensor_copy(out=k.gqb[jf][:], in_=k.ps_mm[2][:]))

            for kt2 in range(0, NT, 2):
                PT_SCP = [0, 0]

                @blk.tensor
                def _(tn, s=s, c2=c2, kt2=kt2, PT_SCP=PT_SCP):
                    for q in range(2):
                        kt = kt2 + q
                        if pa_sc[q]:
                            tn.wait_ge(pa.h, pa_sc[q])
                        kA = k.kT[0:64,
                                  c2 * T + kt * 128:c2 * T + (kt + 1) * 128]
                        kB = k.kT[64:128,
                                  c2 * T + kt * 128:c2 * T + (kt + 1) * 128]
                        qA = k.qT[0:64, c2 * T + s * SW:c2 * T + (s + 1) * SW]
                        qB = k.qT[64:128,
                                  c2 * T + s * SW:c2 * T + (s + 1) * SW]
                        tn.matmul(k.ps_sc[q][:, 0:SW], kA, qA, start=True,
                                  stop=True, tile_position=(0, 0))
                        PT_SCP[q] = pt.inc(tn.matmul(
                            k.ps_sc[q][:, SW:2 * SW], kB, qB,
                            start=True, stop=True, tile_position=(64, 0)))

                PA_MODP = [0, 0]

                @blk.scalar
                def _(sc, s=s, c2=c2, kt2=kt2, PT_SCP=PT_SCP,
                      PA_MODP=PA_MODP):
                    sc.wait_ge(pt.h, PT_SCP[1])
                    for q in range(2):
                        jm = (kt2 + q) % 3
                        if pt_fus[jm]:
                            sc.wait_ge(pt.h, pt_fus[jm])
                    for q in range(2):
                        jm = (kt2 + q) % 3
                        a = sc.activation(
                            out=k.modb[jm][:], in_=k.ps_sc[q][:],
                            func=Act.Sigmoid, bias=-8.1, scale=30.0)
                        pa_sc[q] = pa.inc(a)
                        PA_MODP[q] = pa_sc[q]

                @blk.tensor
                def _(tn, s=s, c2=c2, kt2=kt2, PA_MODP=PA_MODP):
                    for q in range(2):
                        kt = kt2 + q
                        jm = kt % 3
                        tn.wait_ge(pa.h, PA_MODP[q])
                        if kt == 0 and pv_fev[0]:
                            tn.wait_ge(pv.h, pv_fev[0])
                        vA = k.vtm[:, kt * D + c2 * 128:
                                   kt * D + c2 * 128 + 64]
                        vB = k.vtm[:, kt * D + c2 * 128 + 64:
                                   kt * D + (c2 + 1) * 128]
                        tn.matmul(k.ps_mm[0][0:64, :], vA,
                                  k.modb[jm][:, 0:SW],
                                  start=(kt == 0), stop=(kt == NT - 1),
                                  tile_position=(0, 0))
                        pt_fus[jm] = pt.inc(tn.matmul(
                            k.ps_mm[1][64:128, :], vB,
                            k.modb[jm][:, SW:2 * SW],
                            start=(kt == 0), stop=(kt == NT - 1),
                            tile_position=(0, 64)))

            PT_F = pt.v

            @blk.vector
            def _(v, s=s, c2=c2):
                v.wait_ge(pt.h, PT_F)
                v.tensor_tensor(out=fus_ap(s, c2, 0, 64),
                                in0=k.ps_mm[0][0:64, :],
                                in1=k.gqb[jf][0:64, :], op=Alu.mult)
                pv_fev[0] = pv.inc(v.tensor_tensor(
                    out=fus_ap(s, c2, 64, 128),
                    in0=k.ps_mm[1][64:128, :],
                    in1=k.gqb[jf][64:128, :], op=Alu.mult))

            if pending_wo:
                pending_wo.pop(0)()

        PV_FS = pv.v

        def make_wo(s, t4, half, tt_g, PV_FS):
            def emit():
                @blk.tensor
                def _(tn, s=s, t4=t4, half=half):
                    tn.wait_ge(pv.h, PV_FS)
                    if s == 0 and t4 == 0 and half == 0:
                        tn.wait_ge(gd.h, k.GD_WO)
                    if pv_oev[0]:
                        tn.wait_ge(pv.h, pv_oev[0])
                    for c2 in range(NCH):
                        mm = tn.matmul(
                            k.ps_mm[3][:],
                            fus_ap(s, c2, 0, 128)[:, t4 * 128:(t4 + 1) * 128],
                            k.wob[:, c2 * D + half * SW:c2 * D + (half + 1) * SW],
                            start=(c2 == 0), stop=(c2 == NCH - 1))
                    pt.inc(mm)
                PT_O = pt.v

                @blk.vector
                def _(v, s=s, t4=t4, half=half, tt_g=tt_g):
                    v.wait_ge(pt.h, PT_O)
                    if s == 0 and t4 == 0 and half == 0:
                        v.wait_ge(gd.h, GD_BO)
                    pv_oev[0] = pv.inc(v.tensor_tensor(
                        out=k.acc[:, tt_g * D + half * SW:
                                  tt_g * D + (half + 1) * SW],
                        in0=k.ps_mm[3][:],
                        in1=k.bo_b[:, half * SW:(half + 1) * SW],
                        op=Alu.add))
                    PV_ACC[tt_g] = pv_oev[0]
            return emit

        for t4 in range(4):
            for half in range(2):
                pending_wo.append(make_wo(s, t4, half, s * 4 + t4, PV_FS))

    while pending_wo:
        pending_wo.pop(0)()

    k.PV_ACC = PV_ACC
    k.PV_P2 = pv.v
    k.PT_P2 = pt.v


def emit_p3(k):
    nc, blk = k.nc, k.blk
    ld, gd, so, pv, pa, pt, pg = (k.ld, k.gd, k.so, k.pv, k.pa, k.pt, k.pg)
    PT_P2, PV_P2 = k.PT_P2, k.PV_P2

    @blk.vector
    def _(v):
        v.wait_ge(pv.h, k.PV_WSUM)
        rv = k.rowv
        v.tensor_scalar(out=rv[0:1, 0:3], in0=k.wsum[0:1, 0:3],
                        scalar1=1.0 / (D * INNER), scalar2=None, op0=Alu.mult)
        v.drain()
        v.tensor_scalar(out=rv[0:1, 0:3], in0=rv[0:1, 0:3], scalar1=1e-5,
                        scalar2=None, op0=Alu.max)
        v.drain()
        v.tensor_scalar(out=rv[0:1, 3:6], in0=rv[0:1, 0:3], scalar1=0.5,
                        scalar2=None, op0=Alu.mult)
        v.drain()
        v.tensor_scalar(out=rv[0:1, 6:9], in0=rv[0:1, 3:6], scalar1=-1.0,
                        scalar2=None, op0=Alu.mult)
        pv.inc(v.tensor_tensor(out=rv[0:1, 9:10], in0=rv[0:1, 0:1],
                               in1=rv[0:1, 1:2], op=Alu.mult))
    PV_ROW = pv.v

    @blk.tensor
    def _(tn):
        tn.wait_ge(pv.h, PV_ROW)
        pt.inc(tn.matmul(k.ps_mm[2][:, 0:12], k.onesr[:], k.rowv[:],
                         start=True, stop=True))
    PT_ROW = pt.v

    @blk.vector
    def _(v):
        v.wait_ge(pt.h, PT_ROW)
        v.tensor_copy(out=k.frep[:], in_=k.ps_mm[2][:, 0:12])
        v.drain()
        v.tensor_scalar(out=k.cgsx[:], in0=k.sxi[:],
                        scalar1=k.frep[:, 0:1], scalar2=None, op0=Alu.mult)
        v.tensor_tensor(out=k.kfac[:], in0=k.sxi[:], in1=k.sxi[:],
                        op=Alu.mult)
        v.drain()
        v.tensor_scalar(out=k.kfac[:], in0=k.kfac[:],
                        scalar1=k.frep[:, 9:10], scalar2=None, op0=Alu.mult)
        pv.inc(v.tensor_tensor(out=k.rs1sx[:], in0=k.rs1[:], in1=k.sxc[:],
                               op=Alu.mult))
    PV_FC = pv.v

    def wq_dram(mat, c, it):
        d = [k.wgq_d, k.wuq_d, k.wdq_d][mat]
        if mat < 2:
            return d[c * 128:(c + 1) * 128, it * SW:(it + 1) * SW]
        return d[c * 128:(c + 1) * 128, it * SW:(it + 1) * SW]

    def wf_dram(mat, c, it):
        d = [k.wgT, k.wuT, k.wdT][mat]
        return d[c * 128:(c + 1) * 128, it * SW:(it + 1) * SW]

    pt_slot = [0] * 16
    pv_wfq = [0, 0, 0, 0]
    gd_wq = [0] * 16
    PV_XQD = [0, 0]
    PA_XREL = [0, 0]
    LD_XT = [0, 0]
    PA_XQI = [0, 0]
    pv_pb = [0, 0, 0, 0]
    pv_pd = [0, 0, 0, 0]
    pt_tf = [0, 0]
    LD_X = [0, 0]
    LD_WQ = [0] * 16
    PV_OUT = [0, 0]
    SO_OUT = [0, 0]

    def load_wq_tile(mat, it, slot_base, g):
        for c in range(8):
            slot = slot_base + c
            cc = c if mat < 2 else it * 16 // SW
            if g == 0:
                i2 = (mat * 64 + it * 8 + c) % 4

                @blk.sync
                def _(sy, mat=mat, c=c, it=it, i2=i2, slot=slot):
                    if pv_wfq[i2]:
                        sy.wait_ge(pv.h, pv_wfq[i2])
                    else:
                        sy.wait_ge(pa.h, max(PA_XREL))
                    sy.dma_start(out=k.wfq[i2][:],
                                 in_=wf_dram(mat, c, it)).then_inc(ld.h, 16)
                    ld.v += 16
                LDF = ld.v

                @blk.vector
                def _(v, mat=mat, i2=i2, slot=slot):
                    v.wait_ge(ld.h, LDF)
                    if pt_slot[slot]:
                        v.wait_ge(pt.h, pt_slot[slot])
                    else:
                        v.wait_ge(pv.h, PV_P2)
                        v.wait_ge(pt.h, PT_P2)
                    thr = k.frep[:, 3 + mat:4 + mat]
                    nthr = k.frep[:, 6 + mat:7 + mat]
                    v.tensor_scalar(out=k.qa16[0][:], in0=k.wfq[i2][:],
                                    scalar1=thr, scalar2=None, op0=Alu.is_ge)
                    pv_wfq[i2] = pv.inc(v.tensor_scalar(
                        out=k.qa16[1][:], in0=k.wfq[i2][:],
                        scalar1=nthr, scalar2=None, op0=Alu.is_le))
                    v.wait_ge(pv.h, pv_wfq[i2])
                    pv.inc(v.tensor_tensor(out=k.wqt[slot][:],
                                           in0=k.qa16[0][:],
                                           in1=k.qa16[1][:],
                                           op=Alu.subtract))
                PVQ = pv.v

                @blk.gpsimd
                def _(gp, mat=mat, c=c, it=it, slot=slot):
                    gp.wait_ge(pv.h, PVQ)
                    gp.dma_start(out=wq_dram(mat, c, it),
                                 in_=k.wqt[slot][:]).then_inc(gd.h, 16)
                    gd.v += 16
                k._wq_written[(mat, c, it)] = gd.v
                k._wq_ready[slot] = ("pv", PVQ)
            else:

                @blk.gpsimd
                def _(gp, mat=mat, c=c, it=it, slot=slot):
                    if pt_slot[slot]:
                        gp.wait_ge(pt.h, pt_slot[slot])
                    gp.wait_ge(gd.h, k._wq_written[(mat, c, it)])
                    gp.dma_start(out=k.wqt[slot][:],
                                 in_=wq_dram(mat, c, it)).then_inc(gd.h, 16)
                    gd.v += 16
                k._wq_ready[slot] = ("gd", gd.v)

    k._wq_ready = [None] * 16
    k._wq_written = {}

    for g in range(NG):
        for tt in range(4):
            t = g * 4 + tt
            j = tt % 2

            @blk.sync
            def _(sy, t=t, j=j, g=g):
                if g == 0 and t < 2:
                    sy.wait_ge(pt.h, PT_P2)
                    sy.wait_ge(pv.h, PV_P2)
                elif SO_OUT[j]:
                    sy.wait_ge(so.h, SO_OUT[j])
                if PA_XREL[j]:
                    sy.wait_ge(pa.h, PA_XREL[j])
                sy.dma_start(out=k.xrel[j][:],
                             in_=k.xd[t * 128:(t + 1) * 128, :]).then_inc(
                                 ld.h, 16)
                ld.v += 16
            LD_X[j] = ld.v

            @blk.vector
            def _(v, t=t, j=j, g=g):
                v.wait_ge(ld.h, LD_X[j])
                if g == 0 and t < 2:
                    v.wait_ge(pv.h, PV_FC)
                v.tensor_scalar(out=k.xrel[j][:], in0=k.xrel[j][:],
                                scalar1=k.mv1[:, 2 * t:2 * t + 1],
                                scalar2=k.rs1sx[:, t:t + 1],
                                op0=Alu.subtract, op1=Alu.mult)
                v.drain()
                v.tensor_scalar(out=k.xrel[j][:], in0=k.xrel[j][:],
                                scalar1=MAGIC, scalar2=-MAGIC,
                                op0=Alu.add, op1=Alu.add)
                v.drain()
                pv.inc(v.tensor_scalar(out=k.xrel[j][:], in0=k.xrel[j][:],
                                       scalar1=128.25, scalar2=None,
                                       op0=Alu.add))
            PV_XQ = pv.v

            jt0, jt1 = (2 * tt) % 2, (2 * tt + 1) % 2

            @blk.scalar
            def _(sc, j=j, jt0=jt0, jt1=jt1):
                sc.wait_ge(pv.h, PV_XQ)
                for jt in (jt0, jt1):
                    if PV_XQD[jt]:
                        sc.wait_ge(pv.h, PV_XQD[jt])
                for cc, jt in ((0, jt0), (1, jt1)):
                    a = sc.activation(
                        out=k.xqi[jt][:],
                        in_=k.xrel[j][:, cc * 512:(cc + 1) * 512],
                        func=Act.Copy)
                    PA_XQI[jt] = pa.inc(a)
            for cc, jt in ((0, jt0), (1, jt1)):

                @blk.vector
                def _(v, jt=jt, cc=cc, tt=tt, g=g):
                    v.wait_ge(pa.h, PA_XQI[jt])
                    if g == 0 and tt == 0 and cc == 0:
                        v.wait_ge(pt.h, PT_P2)
                    if LD_XT[jt]:
                        v.wait_ge(ld.h, LD_XT[jt])
                    PV_XQD[jt] = pv.inc(v.tensor_scalar(
                        out=k.xq16[jt][:], in0=k.xqi[jt][:],
                        scalar1=-128.0, scalar2=None, op0=Alu.add))
                PV_XQ16 = pv.v

                @blk.sync
                def _(sy, jt=jt, cc=cc, tt=tt, PV_XQ16=PV_XQ16):
                    sy.wait_ge(pv.h, PV_XQ16)
                    for b in range(4):
                        c = cc * 4 + b
                        sy.dma_start_transpose(
                            k.xqTg[:, c * GW + tt * 128:c * GW + (tt + 1) * 128],
                            k.xq16[jt][:, b * 128:(b + 1) * 128]).then_inc(
                                ld.h, 16)
                    ld.v += 4 * 16
                    LD_XT[jt] = ld.v
            PA_XREL[j] = pa.v

        LD_XQ_DONE = ld.v

        for mat in range(2):
            for it in range(NIT):
                sb = (it % 2) * 8
                load_wq_tile(mat, it, sb, g)

                def gu_mm(tt, jp, mat=mat, it=it, sb=sb, g=g):
                    @blk.tensor
                    def _(tn, mat=mat, it=it, tt=tt, jp=jp, sb=sb):
                        if mat == 0 and it == 0 and tt == 0:
                            tn.wait_ge(ld.h, LD_XQ_DONE)
                        wmax = {}
                        for c in range(8):
                            rd = k._wq_ready[sb + c]
                            if rd is not None:
                                wmax[rd[0]] = max(wmax.get(rd[0], 0), rd[1])
                                k._wq_ready[sb + c] = None
                        for typ, val in wmax.items():
                            tn.wait_ge({"pv": pv, "ld": ld,
                                        "gd": gd}[typ].h, val)
                        for c in range(8):
                            if c == 0 and pv_pb[jp]:
                                tn.wait_ge(pv.h, pv_pb[jp])
                            if c == 0 and pv_pd[jp]:
                                tn.wait_ge(pv.h, pv_pd[jp])
                            mm = tn.matmul(
                                k.ps_mm[jp][:],
                                k.xqTg[:, c * GW + tt * 128:
                                       c * GW + (tt + 1) * 128],
                                k.wqt[sb + c][:],
                                start=(c == 0), stop=(c == 7))
                        pt.inc(mm)
                        for c in range(8):
                            pt_slot[sb + c] = pt.v
                    return pt.v

                if mat == 0:
                    for tth in range(2):
                        PT_P = [0, 0]
                        for q in range(2):
                            tt = 2 * tth + q
                            PT_P[q] = gu_mm(tt, tt)
                        PA_P = [0, 0]

                        @blk.scalar
                        def _(sc, g=g, it=it, tth=tth, PT_P=PT_P,
                              PA_P=PA_P):
                            sc.wait_ge(pt.h, PT_P[1])
                            for q in range(2):
                                js = (2 * tth + q) % 2
                                if k._pa_sg[js]:
                                    sc.wait_ge(pv.h, k._pa_sg[js])
                            for q in range(2):
                                tt = 2 * tth + q
                                t = g * 4 + tt
                                PA_P[q] = pa.inc(sc.activation(
                                    out=k.sgb[tt % 2][:],
                                    in_=k.ps_mm[tt][:],
                                    func=Act.Sigmoid,
                                    scale=k.cgsx[:, t:t + 1]))

                        for q in range(2):
                            tt = 2 * tth + q

                            @blk.vector
                            def _(v, it=it, tt=tt, q=q, PA_P=PA_P):
                                v.wait_ge(pa.h, PA_P[q])
                                pv_pb[tt] = pv.inc(v.scalar_tensor_tensor(
                                    out=k.pbuf[:, tt * INNER + it * SW:
                                               tt * INNER + (it + 1) * SW],
                                    in0=k.ps_mm[tt][:], scalar=1.0,
                                    in1=k.sgb[tt % 2][:], op0=Alu.mult,
                                    op1=Alu.mult))
                                k._pa_sg[tt % 2] = pv_pb[tt]
                else:
                    for tt in range(4):
                        jp = tt
                        PT_GU = gu_mm(tt, jp)

                        @blk.vector
                        def _(v, it=it, tt=tt, jp=jp, PT_GU=PT_GU):
                            v.wait_ge(pt.h, PT_GU)
                            sl = k.pbuf[:, tt * INNER + it * SW:
                                        tt * INNER + (it + 1) * SW]
                            pv_pb[jp] = pv.inc(v.scalar_tensor_tensor(
                                out=sl, in0=k.ps_mm[jp][:], scalar=1.0,
                                in1=sl, op0=Alu.mult, op1=Alu.mult))
        PV_GU = pv.v

        LDH = [[0, 0] for _ in range(4)]
        for tt in range(4):
            t = g * 4 + tt

            @blk.vector
            def _(v, t=t, tt=tt):
                if tt == 0:
                    v.wait_ge(pv.h, PV_GU)
                sl = k.pbuf[:, tt * INNER:(tt + 1) * INNER]
                v.reduce_max(out=k.scr[:, 11:12], in_=sl, axis=AX,
                             apply_absolute_value=True)
                v.drain()
                v.tensor_scalar(out=k.scr[:, 11:12], in0=k.scr[:, 11:12],
                                scalar1=k.kfac[:, t:t + 1], scalar2=None,
                                op0=Alu.mult)
                v.drain()
                v.tensor_scalar(out=k.scr[:, 11:12], in0=k.scr[:, 11:12],
                                scalar1=1e-5, scalar2=None, op0=Alu.max)
                v.drain()
                v.reciprocal(out=k.scr[:, 12:13], in_=k.scr[:, 11:12])
                v.drain()
                v.tensor_scalar(out=k.scr[:, 12:13], in0=k.scr[:, 12:13],
                                scalar1=127.0,
                                scalar2=k.kfac[:, t:t + 1],
                                op0=Alu.mult, op1=Alu.mult)
                v.drain()
                v.tensor_scalar(out=k.cdsh[:, tt:tt + 1],
                                in0=k.scr[:, 11:12],
                                scalar1=k.frep[:, 2:3],
                                scalar2=1.0 / 127.0,
                                op0=Alu.mult, op1=Alu.mult)
                v.tensor_scalar(out=sl, in0=sl,
                                scalar1=k.scr[:, 12:13], scalar2=MAGIC,
                                op0=Alu.mult, op1=Alu.add)
                v.drain()
                pv.inc(v.tensor_scalar(out=sl, in0=sl, scalar1=-MAGIC,
                                       scalar2=128.25, op0=Alu.add,
                                       op1=Alu.add))
            PV_H = pv.v

            for sc8 in range(0, 8, 2):
                @blk.scalar
                def _(s_, tt=tt, sc8=sc8):
                    s_.wait_ge(pv.h, PV_H)
                    for q in range(2):
                        jt = (8 * tt + sc8 + q) % 2
                        if PV_XQD[jt]:
                            s_.wait_ge(pv.h, PV_XQD[jt])
                    for q in range(2):
                        jt = (8 * tt + sc8 + q) % 2
                        sl = k.pbuf[:, tt * INNER + (sc8 + q) * SW:
                                    tt * INNER + (sc8 + q + 1) * SW]
                        a = s_.activation(out=k.xqi[jt][:], in_=sl,
                                          func=Act.Copy)
                        PA_XQI[jt] = pa.inc(a)

                for q in range(2):
                    jt = (8 * tt + sc8 + q) % 2

                    @blk.vector
                    def _(v, jt=jt):
                        v.wait_ge(pa.h, PA_XQI[jt])
                        if LD_XT[jt]:
                            v.wait_ge(ld.h, LD_XT[jt])
                        PV_XQD[jt] = pv.inc(v.tensor_scalar(
                            out=k.xq16[jt][:], in0=k.xqi[jt][:],
                            scalar1=-128.0, scalar2=None, op0=Alu.add))
                    PV_HB = pv.v

                    @blk.sync
                    def _(sy, tt=tt, sc8=sc8, q=q, jt=jt, PV_HB=PV_HB):
                        sy.wait_ge(pv.h, PV_HB)
                        for b in range(4):
                            ch = (sc8 + q) * 4 + b
                            sy.dma_start_transpose(
                                k.hT[:, ch * GW + tt * 128:
                                     ch * GW + (tt + 1) * 128],
                                k.xq16[jt][:, b * 128:(b + 1) * 128]
                            ).then_inc(ld.h, 16)
                        ld.v += 4 * 16
                        LD_XT[jt] = ld.v
                LDH[tt][sc8 // 4] = ld.v
        LD_H_DONE = ld.v
        PG_H = pg.v

        for ot in range(2):
            for chh in range(2):
                for c16 in range(16):
                    ch = chh * 16 + c16
                    slot = c16
                    it_eq = ot
                    if g == 0:
                        i2 = (ch + ot * 32) % 4

                        @blk.sync
                        def _(sy, ch=ch, ot=ot, i2=i2):
                            if pv_wfq[i2]:
                                sy.wait_ge(pv.h, pv_wfq[i2])
                            sy.dma_start(out=k.wfq[i2][:],
                                         in_=wf_dram(2, ch, ot)).then_inc(
                                             ld.h, 16)
                            ld.v += 16
                        LDF = ld.v

                        @blk.vector
                        def _(v, ch=ch, ot=ot, i2=i2, slot=slot):
                            v.wait_ge(ld.h, LDF)
                            if pt_slot[slot]:
                                v.wait_ge(pt.h, pt_slot[slot])
                            v.tensor_scalar(out=k.qa16[0][:],
                                            in0=k.wfq[i2][:],
                                            scalar1=k.frep[:, 5:6],
                                            scalar2=None, op0=Alu.is_ge)
                            pv_wfq[i2] = pv.inc(v.tensor_scalar(
                                out=k.qa16[1][:], in0=k.wfq[i2][:],
                                scalar1=k.frep[:, 8:9],
                                scalar2=None, op0=Alu.is_le))
                            v.wait_ge(pv.h, pv_wfq[i2])
                            pv.inc(v.tensor_tensor(out=k.wqt[slot][:],
                                                   in0=k.qa16[0][:],
                                                   in1=k.qa16[1][:],
                                                   op=Alu.subtract))
                        PVQ = pv.v

                        @blk.gpsimd
                        def _(gp, ch=ch, ot=ot, slot=slot):
                            gp.wait_ge(pv.h, PVQ)
                            gp.dma_start(out=wq_dram(2, ch, ot),
                                         in_=k.wqt[slot][:]).then_inc(gd.h, 16)
                            gd.v += 16
                        k._wq_written[(2, ch, ot)] = gd.v
                        k._wq_ready[slot] = ("pv", PVQ)
                    else:

                        @blk.gpsimd
                        def _(gp, ch=ch, ot=ot, slot=slot):
                            if pt_slot[slot]:
                                gp.wait_ge(pt.h, pt_slot[slot])
                            gp.wait_ge(gd.h, k._wq_written[(2, ch, ot)])
                            gp.dma_start(out=k.wqt[slot][:],
                                         in_=wq_dram(2, ch, ot)).then_inc(
                                             gd.h, 16)
                            gd.v += 16
                        k._wq_ready[slot] = ("gd", gd.v)

                for tt in range(4):

                    @blk.tensor
                    def _(tn, g=g, ot=ot, chh=chh, tt=tt):
                        if ot == 0:
                            tn.wait_ge(ld.h, LDH[tt][chh])
                        if tt == 0:
                            wmax = {}
                            for c16 in range(16):
                                rd = k._wq_ready[c16]
                                if rd is not None:
                                    wmax[rd[0]] = max(wmax.get(rd[0], 0),
                                                      rd[1])
                            for typ, val in wmax.items():
                                tn.wait_ge({"pv": pv, "ld": ld, "gd": gd}[typ].h, val)
                        for c16 in range(16):
                            ch = chh * 16 + c16
                            if c16 == 0 and chh == 0 and pv_pd[tt]:
                                tn.wait_ge(pv.h, pv_pd[tt])
                            mm = tn.matmul(
                                k.ps_mm[tt][:],
                                k.hT[:, ch * GW + tt * 128:
                                     ch * GW + (tt + 1) * 128],
                                k.wqt[c16][:],
                                start=(chh == 0 and c16 == 0),
                                stop=(chh == 1 and c16 == 15))
                        if tt == 3:
                            for c16 in range(16):
                                k._wq_ready[c16] = None
                        pt.inc(mm)
                        for c16 in range(16):
                            pt_slot[c16] = pt.v
                    PT_D = pt.v

                    if chh == 1:

                        @blk.vector
                        def _(v, g=g, ot=ot, tt=tt):
                            t = g * 4 + tt
                            v.wait_ge(pt.h, PT_D)
                            asl = k.acc[:, t * D + ot * SW:
                                        t * D + (ot + 1) * SW]
                            pv_pd[tt] = pv.inc(v.scalar_tensor_tensor(
                                out=asl, in0=k.ps_mm[tt][:],
                                scalar=k.cdsh[:, tt:tt + 1], in1=asl,
                                op0=Alu.mult, op1=Alu.add))
        PV_D = pv.v

        for tt in range(4):
            t = g * 4 + tt
            j = tt % 2

            @blk.sync
            def _(sy, t=t, j=j, tt=tt):
                sy.wait_ge(pv.h, PV_D)
                if SO_OUT[j]:
                    sy.wait_ge(so.h, SO_OUT[j])
                sy.dma_start(out=k.xrel[j][:],
                             in_=k.xd[t * 128:(t + 1) * 128, :]).then_inc(
                                 ld.h, 16)
                ld.v += 16
            LD_XF = ld.v

            @blk.vector
            def _(v, t=t, j=j):
                v.wait_ge(ld.h, LD_XF)
                pv.inc(v.tensor_tensor(out=k.xrel[j][:], in0=k.xrel[j][:],
                                       in1=k.acc[:, t * D:(t + 1) * D],
                                       op=Alu.add))
            PV_O = pv.v
            PV_OUT[j] = PV_O

            @blk.sync
            def _(sy, t=t, j=j):
                sy.wait_ge(pv.h, PV_O)
                sy.dma_start(out=k.yd[t * 128:(t + 1) * 128, :],
                             in_=k.xrel[j][:]).then_inc(so.h, 16)
                so.v += 16
            SO_OUT[j] = so.v

    @blk.sync
    def _(sy):
        sy.wait_ge(so.h, so.v)
        sy.wait_ge(gd.h, gd.v)


def emit_taps(k):
    nc, blk = k.nc, k.blk
    so, pv, pa, pt, pg = k.so, k.pv, k.pa, k.pt, k.pg
    src_map = lambda: {
        "xnT": k.xnT, "xmT": k.xmT, "mv1": k.mv1,
        "rs1": k.rs1, "sxc": k.sxc, "sxi": k.sxi,
        "kT": k.kT, "qT": k.qT, "vtm": k.vtm, "acc": k.acc,
        "gq_all": k.gq_all, "gk_all": k.gk_all, "gk_tok": k.gk_tok,
        "idnf": k.idnf, "e2f": k.e2f,
        "fus": k.fus, "wsum": k.wsum, "sxi": k.sxi,
    }

    @blk.sync
    def _(sy):
        sy.wait_ge(pv.h, pv.v)
        sy.wait_ge(pa.h, pa.v)
        sy.wait_ge(pt.h, pt.v)
        sy.wait_ge(pg.h, pg.v)
        m = src_map()
        for name, ap in k.taps.items():
            sy.dma_start(out=ap, in_=m[name][:]).then_inc(so.h, 16)
            so.v += 16
        sy.wait_ge(so.h, so.v)


def kernel(**inputs):
    xs = np.ascontiguousarray(np.asarray(inputs["L0_attn_x"], np.float32))
    B = xs.shape[0]
    base = make_base_inputs(inputs)
    nc, _ = build_program()
    in_maps = [dict(base, x=np.ascontiguousarray(xs[c])) for c in range(B)]
    res = run_bass_kernel_spmd(nc, in_maps, core_ids=list(range(B)))
    return np.stack([res.results[c]["y"] for c in range(B)], axis=0)


def make_base_inputs(inputs):
    return {
        "w_q": np.ascontiguousarray(inputs["L0_attn_w_q"], dtype=np.float32),
        "w_k": np.ascontiguousarray(inputs["L0_attn_w_k"], dtype=np.float32),
        "w_v": np.ascontiguousarray(inputs["L0_attn_w_v"], dtype=np.float32),
        "w_o": np.ascontiguousarray(inputs["L0_attn_w_o"], dtype=np.float32),
        "g_q": np.ascontiguousarray(
            np.asarray(inputs["L0_attn_g_q"], np.float32).reshape(HD, 1)),
        "g_k": np.ascontiguousarray(
            np.asarray(inputs["L0_attn_g_k"], np.float32).reshape(HD, 1)),
        "b_o": np.ascontiguousarray(
            np.asarray(inputs["L0_attn_b_o"], np.float32).reshape(1, D)),
        "gamma": np.ascontiguousarray(
            np.asarray(inputs["L0_attn_gamma"], np.float32).reshape(1, D)),
        "beta": np.ascontiguousarray(
            np.asarray(inputs["L0_attn_beta"], np.float32).reshape(1, D)),
        "wgT": np.ascontiguousarray(
            np.asarray(inputs["L0_ffn_w_gate"], np.float32).T),
        "wuT": np.ascontiguousarray(
            np.asarray(inputs["L0_ffn_w_up"], np.float32).T),
        "wdT": np.ascontiguousarray(
            np.asarray(inputs["L0_ffn_w_down"], np.float32).T),
    }


if __name__ == "__main__":
    print("kernel module ok")


# version 11
# speedup vs baseline: 1.0411x; 1.0411x over previous
import sys
if "/opt/trn_rl_repo" not in sys.path:
    sys.path.insert(0, "/opt/trn_rl_repo")

from contextlib import ExitStack

import numpy as np

import concourse.bass as bass
import concourse.mybir as mybir
from concourse.bass_utils import run_bass_kernel_spmd

F32 = mybir.dt.float32
BF16 = mybir.dt.bfloat16
I32 = mybir.dt.int32
Alu = mybir.AluOpType
Act = mybir.ActivationFunctionType
AX = mybir.AxisListType.X

T = 2048
D = 1024
H = 16
HD = 64
INNER = 4096
NT = T // 128
NCH = D // 128
NST = 4
SW = 512
NG = 4
GW = 512
NIT = INNER // 512
NICH = INNER // 128
MAGIC = 12582912.0
LN_EPS = 1e-5

BUILD_STAGE = 4
P0_TILES = NT
P1_PARTS = ("k", "q", "gkt", "v", "wm")


def _seal(eng, sem):
    eng.drain()
    return sem.inc(eng.nop())


class Sem:
    def __init__(self, ctx, nc, name):
        self.h = ctx.enter_context(nc.semaphore(name))
        self.v = 0

    def inc(self, instr, n=1):
        instr.then_inc(self.h, n)
        self.v += n
        return self.v


class K:
    pass


def build_program(taps=()):
    nc = bass.Bass("TRN2", target_bir_lowering=False, debug=False,
                   detect_race_conditions=False)
    k = K()
    k.nc = nc
    k.taps = {}

    k.xd = nc.dram_tensor("x", [T, D], F32, kind="ExternalInput").ap()
    k.w_q = nc.dram_tensor("w_q", [D, D], F32, kind="ExternalInput").ap()
    k.w_k = nc.dram_tensor("w_k", [D, D], F32, kind="ExternalInput").ap()
    k.w_v = nc.dram_tensor("w_v", [D, D], F32, kind="ExternalInput").ap()
    k.w_o = nc.dram_tensor("w_o", [D, D], F32, kind="ExternalInput").ap()
    k.g_q = nc.dram_tensor("g_q", [HD, 1], F32, kind="ExternalInput").ap()
    k.g_k = nc.dram_tensor("g_k", [HD, 1], F32, kind="ExternalInput").ap()
    k.b_o = nc.dram_tensor("b_o", [1, D], F32, kind="ExternalInput").ap()
    k.gamma = nc.dram_tensor("gamma", [1, D], F32, kind="ExternalInput").ap()
    k.beta = nc.dram_tensor("beta", [1, D], F32, kind="ExternalInput").ap()
    k.wgT = nc.dram_tensor("wgT", [D, INNER], F32, kind="ExternalInput").ap()
    k.wuT = nc.dram_tensor("wuT", [D, INNER], F32, kind="ExternalInput").ap()
    k.wdT = nc.dram_tensor("wdT", [INNER, D], F32, kind="ExternalInput").ap()
    k.yd = nc.dram_tensor("y", [T, D], F32, kind="ExternalOutput").ap()
    k.wgq_d = nc.dram_tensor("wgq_d", [D, INNER], BF16, kind="Internal").ap()
    k.wuq_d = nc.dram_tensor("wuq_d", [D, INNER], BF16, kind="Internal").ap()
    k.wdq_d = nc.dram_tensor("wdq_d", [INNER, D], BF16, kind="Internal").ap()
    k.rowscr = nc.dram_tensor("rowscr", [1, 4096], F32, kind="Internal").ap()
    for name, shape, dt in taps:
        k.taps[name] = nc.dram_tensor("tap_" + name, shape, dt,
                                      kind="ExternalOutput").ap()

    ctx = ExitStack()
    e = ctx.enter_context
    with ctx:
        for s in ["ld", "so", "gd", "pv", "pa", "pt", "pg"]:
            setattr(k, s, Sem(ctx, nc, s))

        base = (nc.sbuf_base + 31) // 32 * 32
        top = nc.sbuf_top

        def at(name, shape, dt, off):
            return nc.alloc_sbuf_tensor_at(name, shape, dt, offset=base + off)

        k.xnT = at("xnT", [128, NCH * T], BF16, 0)
        k.xmT = at("xmT", [128, NCH * T], BF16, 32768)
        k.acc = at("acc", [128, NT * D], F32, 0)
        k.kT = at("kT", [128, NCH * T], BF16, 65536)
        k.qT = at("qT", [128, NCH * T], BF16, 98304)
        k.vtm = at("vtm", [128, NT * D], BF16, 131072)
        k.pbuf = at("pbuf", [128, GW * INNER // 128], F32, 65536)
        k.hT = at("hT", [128, NICH * GW], BF16, 131072)
        C = 160 * 1024
        Dof_wm = 184 * 1024 + 8192
        k.xb = [at(f"xb{i}", [128, D], F32, C + 4096 * i) for i in range(2)]
        k.xnb = at("xnb", [128, D], F32, C + 12288)
        k.xmb = at("xmb", [128, D], F32, C + 16384)
        _cb_off = [20480, 22528, 8192, 10240]
        k.cb16 = [at(f"cb16_{i}", [128, D], BF16, C + _cb_off[i])
                  for i in range(4)]
        DW = 2 * SW
        k.rvb16 = at("rvb16", [2, DW], BF16, C + 0)
        k.wsb = [at(f"wsb{i}", [128, D], BF16, C + 2048 + 2048 * i)
                 for i in range(2)]
        k.sqb = [at(f"sqb{i}", [128, DW], BF16, C + 6144 + 2048 * i)
                 for i in range(2)]
        k.rb_sb = at("rb_sb", [128, DW], F32, C + 10240)
        k.rvrow = at("rvrow", [2, DW], F32, C + 14336)
        k.gsb = at("gsb", [2, DW], F32, C + 18432)
        k.wmb = [at(f"wmb{i}", [128, SW], F32, Dof_wm + 2048 * i)
                 for i in range(2)]
        k.wvb = at("wvb", [128, NCH * D], BF16, 32768)
        k.modb = [at(f"modb{i}", [128, 2 * SW], BF16, C + 2048 * i)
                  for i in range(3)]
        k.gqb = [at(f"gqb{i}", [128, SW], F32, C + 6144 + 2048 * i)
                 for i in range(2)]
        k.fus = at("fus", [128, NCH * SW], BF16, C + 10240)
        k.e16 = at("e16", [16, NCH * 128], BF16, C + 18432)
        k.e16i = at("e16i", [16, 128], I32, C + 20480)
        k.xrel = [at(f"xrel{i}", [128, D], F32, C + 4096 * i) for i in range(2)]
        k.xqi = [at(f"xqi{i}", [128, SW], I32, C + 8192 + 2048 * i)
                 for i in range(2)]
        k.xqTg = at("xqTg", [128, NCH * GW], BF16, C + 12288)
        k.sgb = [at(f"sgb{i}", [128, SW], F32, C + 20480 + 2048 * i)
                 for i in range(2)]
        k.wfq = [at(f"wfq{i}", [128, SW], F32, C + 2048 * i)
                 for i in range(4)]
        k.xq16 = [at(f"xq16_{i}", [128, SW], BF16, C + 20480 + 1024 * i)
                  for i in range(2)]
        k.qa16 = [at(f"qa16_{i}", [128, SW], BF16, C + 22528 + 1024 * i)
                  for i in range(2)]
        k._pa_sg = [0, 0]
        Dof = 184 * 1024
        k.gk_all = at("gk_all", [16, T], F32, Dof)
        k.wob = at("wob", [128, NCH * D], BF16, Dof)
        k.wqt = [at(f"wqt{i}", [128, SW], BF16, Dof + 1024 * i)
                 for i in range(16)]
        E = 200 * 1024
        off = E

        def nxt(name, shape, dt):
            nonlocal off
            t_ = at(name, shape, dt, off)
            nbytes = int(np.prod(shape[1:])) * mybir.dt.size(dt)
            off += (nbytes + 31) // 32 * 32
            return t_

        k.mv1 = nxt("mv1", [128, 2 * NT], F32)
        k.rs1 = nxt("rs1", [128, NT], F32)
        k.sxc = nxt("sxc", [128, NT], F32)
        k.sxi = nxt("sxi", [128, NT], F32)
        k.scr = nxt("scr", [128, 16], F32)
        k.bns = nxt("bns", [128, 12], F32)
        k.ind1b = nxt("ind1b", [128, 2], BF16)
        k.indgq = nxt("indgq", [128, 2], BF16)
        k.indgk = nxt("indgk", [128, 2], BF16)
        k.e2f = nxt("e2f", [2, 128], F32)
        k.e2b = nxt("e2b", [2, 128], BF16)
        k.onesf = nxt("onesf", [128, 1], F32)
        k.onesbf = nxt("onesbf", [128, 1], BF16)
        k.onesr = nxt("onesr", [1, 128], F32)
        k.idnb = nxt("idnb", [128, 128], BF16)
        k.idnf = nxt("idnf", [128, 128], F32)
        k.cvals = nxt("cvals", [128, 1], F32)
        nc.const_aps.register(F32, -8.1, k.cvals[:, 0:1])
        k.gq_all = nxt("gq_all", [16, T], BF16)
        k.gk_tok = nxt("gk_tok", [128, NT * 16], BF16)
        k.cg_c = nxt("cg_c", [128, 1], F32)
        k.cu_c = nxt("cu_c", [128, 1], F32)
        k.cd_c = nxt("cd_c", [128, 1], F32)
        k.thr_g = nxt("thr_g", [128, 1], F32)
        k.thr_u = nxt("thr_u", [128, 1], F32)
        k.thr_d = nxt("thr_d", [128, 1], F32)
        k.cgsx = nxt("cgsx", [128, NT], F32)
        k.kfac = nxt("kfac", [128, NT], F32)
        k.wmac = nxt("wmac", [128, 4], F32)
        k.rowv = nxt("rowv", [1, 12], F32)
        k.frep = nxt("frep", [128, 12], F32)
        k.rs1sx = nxt("rs1sx", [128, NT], F32)
        k.cdsh = nxt("cdsh", [128, 4], F32)
        k.wsum = nxt("wsum", [1, 4], F32)
        k.io32 = at("io32", [128, 128], I32, C + 0)
        k.bo_b = at("bo_b", [128, D], BF16, C + 22528)
        k.gam_b = at("gam_b", [128, D], F32, 65536)
        k.bet_b = at("bet_b", [128, D], F32, 65536 + 4096)
        assert off <= top - base, (off, top - base)

        pat = lambda name, shape, dt, bank: nc.alloc_psum_tensor(
            name, shape, dt)
        psum_base0 = nc.psum_base
        k.ps_mm = [pat(f"ps_mm{i}", [128, 512], F32, i) for i in range(4)]
        psum_base = nc.psum_base
        nc.psum_base = psum_base0
        k.ps_kq = [pat(f"ps_kq{i}", [128, 1024], F32, 2 * i)
                   for i in range(2)]
        nc.psum_base = psum_base
        k.ps_tp = [pat(f"ps_tp{i}", [128, 1024], BF16, 4 + i)
                   for i in range(2)]
        k.ps_ss = [pat(f"ps_ss{i}", [2, 512], F32, 6 + i) for i in range(2)]
        nc.psum_base = psum_base
        k.ps_sc = [pat(f"ps_sc{i}", [128, 1024], F32, 4 + 2 * i)
                   for i in range(2)]
        nc.psum_base = psum_base
        k.ps_tf = [pat(f"ps_tf{i}", [128, 512], F32, 4 + i) for i in range(2)]
        nc.psum_base = psum_base + 4

        with nc.Block() as blk:
            k.blk = blk
            emit_setup(k)
            if BUILD_STAGE >= 1:
                emit_p0(k)
            if BUILD_STAGE >= 2:
                emit_p1(k)
            if BUILD_STAGE >= 3:
                emit_p2(k)
            if BUILD_STAGE >= 4:
                emit_p3(k)
            emit_taps(k)

    return nc, k


def emit_setup(k):
    nc, blk = k.nc, k.blk
    ld, pv, pg = k.ld, k.pv, k.pg

    @blk.sync
    def _(sy):
        sy.dma_start(out=k.gam_b[:].unsqueeze(1),
                     in_=k.gamma.partition_broadcast(128)).then_inc(ld.h, 16)
        sy.dma_start(out=k.bet_b[:].unsqueeze(1),
                     in_=k.beta.partition_broadcast(128)).then_inc(ld.h, 16)
        ld.v += 2 * 16
    k.LD_SETUP = ld.v

    @blk.gpsimd
    def _(gp):
        gd = k.gd
        pg.inc(gp.iota(k.io32[:], [[-1, 128]], base=0, channel_multiplier=1))
        gp.dma_start(out=k.indgq[0:64, 0:1], in_=k.g_q).then_inc(gd.h, 16)
        gp.dma_start(out=k.indgq[64:128, 1:2], in_=k.g_q).then_inc(gd.h, 16)
        gp.dma_start(out=k.indgk[0:64, 0:1], in_=k.g_k).then_inc(gd.h, 16)
        gp.dma_start(out=k.indgk[64:128, 1:2], in_=k.g_k).then_inc(gd.h, 16)
        gd.v += 4 * 16
        gp.wait_ge(k.pv.h, 1)
        pg.inc(gp.iota(k.io32[0:2, :], [[1, 2], [0, 64]], base=0,
                       channel_multiplier=-1))
    k.PG_IOTA = pg.v - 1
    k.PG_E2 = pg.v
    k.GD_SETUP = k.gd.v

    @blk.vector
    def _(v):
        v.wait_ge(ld.h, k.LD_SETUP)
        v.wait_ge(k.gd.h, k.GD_SETUP)
        v.memset(k.cvals[:, 0:1], -8.1)
        v.memset(k.ind1b[:], 0.0)
        v.memset(k.indgq[0:64, 1:2], 0.0)
        v.memset(k.indgq[64:128, 0:1], 0.0)
        v.memset(k.indgk[0:64, 1:2], 0.0)
        v.memset(k.indgk[64:128, 0:1], 0.0)
        v.memset(k.ind1b[0:64, 0:1], 1.0)
        v.memset(k.ind1b[64:128, 1:2], 1.0)
        v.memset(k.onesf[:], 1.0)
        v.memset(k.onesbf[:], 1.0)
        v.memset(k.onesr[:], 1.0)
        v.memset(k.wmac[:], 0.0)
        v.memset(k.wsum[:], 0.0)
        v.memset(k.rowv[:], 0.0)
        v.wait_ge(pg.h, k.PG_IOTA)
        v.tensor_scalar(out=k.idnb[:], in0=k.io32[:], scalar1=0.0,
                        scalar2=None, op0=Alu.is_equal)
        pv.inc(v.tensor_scalar(out=k.idnf[:], in0=k.io32[:], scalar1=0.0,
                               scalar2=None, op0=Alu.is_equal))
        v.wait_ge(pg.h, k.PG_E2)
        v.tensor_scalar(out=k.e2f[:], in0=k.io32[0:2, :], scalar1=0.0,
                        scalar2=None, op0=Alu.is_equal)
        pv.inc(v.tensor_scalar(out=k.e2b[:], in0=k.io32[0:2, :], scalar1=0.0,
                               scalar2=None, op0=Alu.is_equal))
    k.PV_CONST = pv.v


def emit_p0(k):
    nc, blk = k.nc, k.blk
    ld, pv, pa, pt = k.ld, k.pv, k.pa, k.pt
    PV_TILE = [0] * NT
    PV_EV = [0] * NT
    LD_TP = [0] * NT

    for t in range(P0_TILES):

        @blk.sync
        def _(sy, t=t):
            if t < 2:
                sy.wait_ge(pv.h, k.PV_CONST)
            else:
                sy.wait_ge(pv.h, PV_TILE[t - 2])
            sy.dma_start(out=k.xb[t % 2][:],
                         in_=k.xd[t * 128:(t + 1) * 128, :]).then_inc(ld.h, 16)
            ld.v += 16
        LD_T = ld.v

        @blk.vector
        def _(v, t=t):
            xt = k.xb[t % 2]
            if t == 0:
                v.wait_ge(pv.h, k.PV_CONST)
            v.wait_ge(ld.h, LD_T)
            v.bn_stats(out=k.bns[:, 0:6], in_=xt[:, 0:512])
            v.bn_stats(out=k.bns[:, 6:12], in_=xt[:, 512:1024])
            v.drain()
            v.bn_aggr(out=k.mv1[:, 2 * t:2 * t + 2], in_=k.bns[:])
            v.drain()
            v.tensor_scalar(
                out=k.scr[:, 0:1], in0=k.mv1[:, 2 * t + 1:2 * t + 2],
                scalar1=LN_EPS, scalar2=None, op0=Alu.add)
            v.drain()
            pv.inc(v.reciprocal(out=k.scr[:, 1:2], in_=k.scr[:, 0:1]))
        PV_R1 = pv.v

        @blk.scalar
        def _(s, t=t):
            s.wait_ge(pv.h, PV_R1)
            pa.inc(s.activation(out=k.rs1[:, t:t + 1], in_=k.scr[:, 1:2],
                                func=Act.Sqrt))
        PA_R1 = pa.v

        @blk.vector
        def _(v, t=t):
            xt = k.xb[t % 2]
            v.wait_ge(pa.h, PA_R1)
            PV_TILE[t] = pv.inc(v.tensor_scalar(
                out=k.xnb[:], in0=xt[:],
                scalar1=k.mv1[:, 2 * t:2 * t + 1],
                scalar2=k.rs1[:, t:t + 1],
                op0=Alu.subtract, op1=Alu.mult))
            v.wait_ge(pv.h, PV_TILE[t])
            v.reduce_max(out=k.scr[:, 2:3], in_=k.xnb[:], axis=AX,
                         apply_absolute_value=True)
            v.drain()
            v.tensor_scalar(out=k.scr[:, 2:3], in0=k.scr[:, 2:3],
                            scalar1=1e-5, scalar2=None, op0=Alu.max)
            v.drain()
            v.tensor_scalar(out=k.sxi[:, t:t + 1], in0=k.scr[:, 2:3],
                            scalar1=1.0 / 127.0, scalar2=None, op0=Alu.mult)
            v.reciprocal(out=k.scr[:, 3:4], in_=k.scr[:, 2:3])
            v.drain()
            v.tensor_scalar(out=k.sxc[:, t:t + 1], in0=k.scr[:, 3:4],
                            scalar1=127.0, scalar2=None, op0=Alu.mult)
            v.bn_stats(out=k.bns[:, 0:6], in_=k.xnb[:, 0:512])
            v.bn_stats(out=k.bns[:, 6:12], in_=k.xnb[:, 512:1024])
            v.drain()
            v.bn_aggr(out=k.scr[:, 6:8], in_=k.bns[:])
            v.drain()
            v.tensor_scalar(out=k.scr[:, 4:5], in0=k.scr[:, 7:8],
                            scalar1=LN_EPS, scalar2=None, op0=Alu.add)
            v.drain()
            pv.inc(v.reciprocal(out=k.scr[:, 5:6], in_=k.scr[:, 4:5]))
        PV_R2 = pv.v

        @blk.scalar
        def _(s, t=t):
            s.wait_ge(pv.h, PV_R2)
            pa.inc(s.activation(out=k.scr[:, 8:9], in_=k.scr[:, 5:6],
                                func=Act.Sqrt))
        PA_R2 = pa.v

        @blk.vector
        def _(v, t=t):
            v.wait_ge(pa.h, PA_R2)
            if t >= 2:
                v.wait_ge(ld.h, LD_TP[t - 2])
            v.tensor_scalar(out=k.xmb[:], in0=k.xnb[:],
                            scalar1=k.scr[:, 6:7],
                            scalar2=k.scr[:, 8:9],
                            op0=Alu.subtract, op1=Alu.mult)
            v.drain()
            v.tensor_tensor(out=k.xmb[:], in0=k.xmb[:], in1=k.gam_b[:],
                            op=Alu.mult)
            v.drain()
            v.tensor_tensor(out=k.xmb[:], in0=k.xmb[:], in1=k.bet_b[:],
                            op=Alu.add)
            v.tensor_copy(out=k.cb16[2 * (t % 2)][:], in_=k.xnb[:])
            v.drain()
            PV_TILE[t] = pv.inc(
                v.tensor_copy(out=k.cb16[2 * (t % 2) + 1][:], in_=k.xmb[:]))
        PV_CAST = pv.v

        @blk.sync
        def _(sy, t=t):
            sy.wait_ge(pv.h, PV_CAST)
            for c in range(NCH):
                sy.dma_start_transpose(
                    k.xnT[:, c * T + t * 128:c * T + (t + 1) * 128],
                    k.cb16[2 * (t % 2)][:, c * 128:(c + 1) * 128]).then_inc(
                        ld.h, 16)
                sy.dma_start_transpose(
                    k.xmT[:, c * T + t * 128:c * T + (t + 1) * 128],
                    k.cb16[2 * (t % 2) + 1][:, c * 128:(c + 1) * 128]
                ).then_inc(ld.h, 16)
            ld.v += 2 * NCH * 16
            LD_TP[t] = ld.v

    k.PV_P0_DONE = pv.v
    k.PT_P0_DONE = pt.v
    k.LD_P0_TP = LD_TP[P0_TILES - 1] if P0_TILES else ld.v


def emit_p1(k):
    nc, blk = k.nc, k.blk
    ld, gd, pv, pa, pt, pg = k.ld, k.gd, k.pv, k.pa, k.pt, k.pg
    PV_P0 = pv.v

    DW = 2 * SW
    NSP = NST // 2
    k._pvev = [0, 0]
    k._gd_gate = 0
    k._pa_g = 0
    k._pa_rv = 0
    k._pt_rb = 0
    k._pvrb = 0
    k._ptw = [0, 0]
    k._it = 0
    ps_rb = k.ps_sc[0]
    ps_ssw = k.ps_sc[1]

    def build_kq(which):
        wsrc = k.w_k if which == "k" else k.w_q
        dstT = k.kT if which == "k" else k.qT
        indg = k.indgk if which == "k" else k.indgq
        gall = k.gk_all if which == "k" else k.gq_all
        wviewed = wsrc.rearrange("(c p) n -> p c n", c=NCH)
        GD_W = [0, 0]
        for c2 in range(NCH):

            @blk.gpsimd
            def _(gp, c2=c2):
                if c2 == 0 and which == "k":
                    gp.wait_ge(pv.h, PV_P0)
                if k._ptw[c2 % 2]:
                    gp.wait_ge(pt.h, k._ptw[c2 % 2])
                gp.dma_start(
                    out=k.wsb[c2 % 2][:].rearrange("p (c n) -> p c n", c=NCH),
                    in_=wviewed[:, :, c2 * 128:(c2 + 1) * 128],
                ).then_inc(gd.h, 16)
                gd.v += 16
                GD_W[c2 % 2] = gd.v

            for sp in range(NSP):
                ping = k._it % 2
                k._it += 1
                ps_k = k.ps_kq[ping]
                sq = k.sqb[ping]

                @blk.tensor
                def _(tn, c2=c2, sp=sp, ping=ping, ps_k=ps_k):
                    if c2 == 0 and sp == 0 and which == "k":
                        tn.wait_ge(pv.h, PV_P0)
                        tn.wait_ge(ld.h, k.LD_P0_TP)
                    tn.wait_ge(gd.h, GD_W[c2 % 2])
                    if k._pvev[ping]:
                        tn.wait_ge(pv.h, k._pvev[ping])
                    wb = k.wsb[c2 % 2]
                    for half in range(2):
                        for ci in range(NCH):
                            mm = tn.matmul(
                                ps_k[:, half * SW:(half + 1) * SW],
                                wb[:, ci * 128:(ci + 1) * 128],
                                k.xmT[:, ci * T + sp * DW + half * SW:
                                      ci * T + sp * DW + (half + 1) * SW],
                                start=(ci == 0), stop=(ci == NCH - 1))
                    pt.inc(mm)
                PT_K = pt.v

                @blk.scalar
                def _(sc, ps_k=ps_k, sq=sq):
                    sc.wait_ge(pt.h, PT_K)
                    pa.inc(sc.activation(out=sq[:], in_=ps_k[:],
                                         func=Act.Square))
                PA_SQ = pa.v

                @blk.tensor
                def _(tn, sq=sq):
                    tn.wait_ge(pa.h, PA_SQ)
                    if k._pa_g:
                        tn.wait_ge(pa.h, k._pa_g)
                    for half in range(2):
                        mm = tn.matmul(
                            ps_ssw[0:2, half * SW:(half + 1) * SW],
                            k.ind1b[:], sq[:, half * SW:(half + 1) * SW],
                            start=True, stop=True)
                    pt.inc(mm)
                PT_SS = pt.v

                @blk.vector
                def _(v):
                    v.wait_ge(pt.h, PT_SS)
                    if k._pa_rv:
                        v.wait_ge(pa.h, k._pa_rv)
                    pv.inc(v.tensor_scalar(out=k.rvrow[:],
                                           in0=ps_ssw[0:2, :],
                                           scalar1=1e-24, scalar2=None,
                                           op0=Alu.max))
                PV_SG = pv.v

                @blk.scalar
                def _(sc):
                    sc.wait_ge(pv.h, PV_SG)
                    sc.activation(out=k.rvrow[:], in_=k.rvrow[:], func=Act.Ln)
                    sc.drain()
                    k._pa_rv = pa.inc(sc.activation(
                        out=k.rvrow[:], in_=k.rvrow[:], func=Act.Exp,
                        scale=-0.5))
                PA_RV = pa.v

                @blk.vector
                def _(v):
                    v.wait_ge(pa.h, PA_RV)
                    if k._pt_rb:
                        v.wait_ge(pt.h, k._pt_rb)
                    pv.inc(v.tensor_scalar(out=k.rvb16[:], in0=k.rvrow[:],
                                           scalar1=1e12, scalar2=None,
                                           op0=Alu.min))
                PV_RV = pv.v

                @blk.tensor
                def _(tn):
                    tn.wait_ge(pv.h, PV_RV)
                    if k._pvrb:
                        tn.wait_ge(pv.h, k._pvrb)
                    for half in range(2):
                        mm = tn.matmul(
                            ps_rb[:, half * SW:(half + 1) * SW], k.e2b[:],
                            k.rvb16[0:2, half * SW:(half + 1) * SW],
                            start=True, stop=True)
                    k._pt_rb = pt.inc(mm)
                PT_RB = pt.v

                @blk.vector
                def _(v, c2=c2, sp=sp, ping=ping, ps_k=ps_k):
                    v.wait_ge(pt.h, PT_RB)
                    k._pvrb = pv.inc(
                        v.tensor_copy(out=k.rb_sb[:], in_=ps_rb[:]))
                    v.wait_ge(pv.h, k._pvrb)
                    k._pvev[ping] = pv.inc(v.tensor_tensor(
                        out=dstT[:, c2 * T + sp * DW:c2 * T + (sp + 1) * DW],
                        in0=ps_k[:], in1=k.rb_sb[:], op=Alu.mult))
                PV_KT = pv.v

                @blk.tensor
                def _(tn, c2=c2, sp=sp):
                    tn.wait_ge(pv.h, PV_KT)
                    for half in range(2):
                        mm = tn.matmul(
                            ps_ssw[0:2, half * SW:(half + 1) * SW], indg[:],
                            dstT[:, c2 * T + sp * DW + half * SW:
                                 c2 * T + sp * DW + (half + 1) * SW],
                            start=True, stop=True)
                    pt.inc(mm)
                PT_DOT = pt.v

                @blk.scalar
                def _(sc):
                    sc.wait_ge(pt.h, PT_DOT)
                    if k._gd_gate:
                        sc.wait_ge(gd.h, k._gd_gate)
                    k._pa_g = pa.inc(sc.activation(out=k.gsb[:],
                                                   in_=ps_ssw[0:2, :],
                                                   func=Act.Sigmoid))
                PA_G = pa.v

                @blk.vector
                def _(v):
                    v.wait_ge(pa.h, PA_G)
                    pv.inc(v.tensor_scalar(
                        out=k.gsb[:], in0=k.gsb[:], scalar1=0.75,
                        scalar2=0.25, op0=Alu.mult, op1=Alu.add))
                PV_GG = pv.v

                @blk.gpsimd
                def _(gp, c2=c2, sp=sp):
                    gp.wait_ge(pv.h, PV_GG)
                    gp.dma_start(
                        out=gall[2 * c2:2 * c2 + 2, sp * DW:(sp + 1) * DW],
                        in_=k.gsb[:]).then_inc(gd.h, 16)
                    gd.v += 16
                    k._gd_gate = gd.v
                for _i3 in range(3):
                    emit_wm_piece()
            k._ptw[c2 % 2] = pt.v

    wm_pieces = []
    for mi, wm in enumerate([k.wgT, k.wuT]):
        wr = wm.rearrange("(c p) n -> c p n", p=128)
        for c in range(NCH):
            for j in range(8):
                wm_pieces.append((mi, wr[c, :, j * 512:(j + 1) * 512]))
    wdr = k.wdT.rearrange("(c p) n -> c p n", p=128)
    for c in range(NICH):
        for j in range(2):
            wm_pieces.append((2, wdr[c, :, j * 512:(j + 1) * 512]))
    LDW_P = [0] * len(wm_pieces)
    PV_WM = [0] * len(wm_pieces)
    k._wm_next = 0

    def emit_wm_piece():
        i = k._wm_next
        if i >= len(wm_pieces) or "wm" not in P1_PARTS:
            return
        k._wm_next += 1
        mi, ap = wm_pieces[i]

        @blk.sync
        def _(sy, i=i, ap=ap):
            if i >= 2:
                sy.wait_ge(pv.h, PV_WM[i - 2])
            sy.dma_start(out=k.wmb[i % 2][:], in_=ap).then_inc(ld.h, 16)
            ld.v += 16
            LDW_P[i] = ld.v

        @blk.vector
        def _(v, i=i, mi=mi):
            v.wait_ge(ld.h, LDW_P[i])
            v.reduce_sum(out=k.scr[:, 10:11], in_=k.wmb[i % 2][:], axis=AX,
                         apply_absolute_value=True)
            v.drain()
            PV_WM[i] = pv.inc(v.tensor_tensor(
                out=k.wmac[:, mi:mi + 1], in0=k.wmac[:, mi:mi + 1],
                in1=k.scr[:, 10:11], op=Alu.add))

    if "k" in P1_PARTS:
        build_kq("k")
    if "q" in P1_PARTS:
        build_kq("q")
    k.PV_KQ = k.pv.v

    if "gkt" not in P1_PARTS:
        k.PV_P1 = pv.v
        k.PT_P1 = pt.v
        k.PV_WSUM = pv.v
        return
    @blk.tensor
    def _(tn):
        tn.wait_ge(pv.h, k.PV_KQ)
        tn.wait_ge(gd.h, gd.v)
        for t in range(NT):
            tn.transpose(k.ps_mm[2][:, t * 16:(t + 1) * 16],
                         k.gk_all[:, t * 128:(t + 1) * 128],
                         k.idnf[0:16, 0:16])
            if t % 4 == 3:
                _seal(tn, pt)
                tn.wait_ge(pt.h, pt.v)
        _seal(tn, pt)
    PT_GKT = pt.v

    @blk.vector
    def _(v):
        v.wait_ge(pt.h, PT_GKT)
        pv.inc(v.tensor_copy(out=k.gk_tok[:], in_=k.ps_mm[2][:, 0:NT * 16]))
    PV_GKT = pv.v

    if "v" not in P1_PARTS:
        k.PV_P1 = pv.v
        k.PT_P1 = pt.v
        k.PV_WSUM = pv.v
        return
    @blk.gpsimd
    def _(gp):
        gp.wait_ge(pv.h, k.PV_KQ)
        wvr = k.w_v.rearrange("(c p) n -> c p n", p=128)
        for c in range(NCH):
            gp.dma_start(out=k.wvb[:, c * D:(c + 1) * D],
                         in_=wvr[c]).then_inc(gd.h, 16)
        gd.v += NCH * 16
    GD_WV = gd.v

    PV_VE = [0, 0, 0, 0]
    for t in range(NT):

        @blk.tensor
        def _(tn, t=t):
            if t == 0:
                tn.wait_ge(gd.h, GD_WV)
                tn.wait_ge(pv.h, PV_GKT)
            for half in range(2):
                ps_v = k.ps_mm[(2 * t + half) % 4]
                if PV_VE[(2 * t + half) % 4]:
                    tn.wait_ge(pv.h, PV_VE[(2 * t + half) % 4])
                for ci in range(NCH):
                    mm = tn.matmul(
                        ps_v[:],
                        k.xnT[:, ci * T + t * 128:ci * T + (t + 1) * 128],
                        k.wvb[:, ci * D + half * SW:ci * D + half * SW + SW],
                        start=(ci == 0), stop=(ci == NCH - 1))
                pt.inc(mm)
        PT_V = pt.v

        @blk.vector
        def _(v, t=t):
            for half in range(2):
                ps_v = k.ps_mm[(2 * t + half) % 4]
                v.wait_ge(pt.h, PT_V - 1 + half)
                gsl = k.gk_tok[:, t * 16 + half * 8:t * 16 + (half + 1) * 8]
                PV_VE[(2 * t + half) % 4] = pv.inc(v.tensor_tensor(
                    out=k.vtm[:, t * D + half * SW:t * D + (half + 1) * SW]
                    .rearrange("p (h e) -> p h e", e=HD),
                    in0=ps_v[:].rearrange("p (h e) -> p h e", e=HD),
                    in1=gsl.unsqueeze(2).broadcast_to([128, 8, HD]),
                    op=Alu.mult))

    k.PV_P1 = pv.v
    k.PT_P1 = pt.v

    if "wm" not in P1_PARTS:
        k.PV_WSUM = pv.v
        return
    while k._wm_next < len(wm_pieces) and "wm" in P1_PARTS:
        emit_wm_piece()

    @blk.tensor
    def _(tn):
        tn.wait_ge(pv.h, PV_WM[-1])
        tn.wait_ge(pv.h, k.PV_P1)
        for mi in range(3):
            mm = tn.matmul(k.ps_ss[0][0:1, mi:mi + 1], k.wmac[:, mi:mi + 1],
                           k.onesf[:], start=True, stop=True)
        pt.inc(mm)
    PT_WSUM = pt.v

    @blk.vector
    def _(v):
        v.wait_ge(pt.h, PT_WSUM)
        pv.inc(v.tensor_copy(out=k.wsum[0:1, 0:3], in_=k.ps_ss[0][0:1, 0:3]))
    k.PV_WSUM = pv.v


def emit_p2(k):
    nc, blk = k.nc, k.blk
    ld, gd, pv, pa, pt, pg = k.ld, k.gd, k.pv, k.pa, k.pt, k.pg
    PV_P1 = k.PV_P1
    GD_P1 = gd.v

    @blk.gpsimd
    def _(gp):
        gp.wait_ge(pt.h, k.PT_P1)
        gp.wait_ge(pv.h, k.PV_WSUM)
        wor = k.w_o.rearrange("(c p) n -> c p n", p=128)
        for c in range(NCH):
            gp.dma_start(out=k.wob[:, c * D:(c + 1) * D],
                         in_=wor[c]).then_inc(gd.h, 16)
        gd.v += NCH * 16
    k.GD_WO = gd.v

    PV_E16 = [0] * NCH
    for c2 in range(NCH):

        @blk.gpsimd
        def _(gp, c2=c2):
            if c2 == 0:
                gp.wait_ge(gd.h, GD_P1)
                gp.wait_ge(pv.h, k.PV_WSUM)
            else:
                gp.wait_ge(pv.h, PV_E16[c2 - 1])
            pg.inc(gp.iota(k.e16i[:], [[1, 2], [0, 64]], base=2 * c2,
                           channel_multiplier=-1))
        PG_E = pg.v

        @blk.vector
        def _(v, c2=c2):
            v.wait_ge(pg.h, PG_E)
            PV_E16[c2] = pv.inc(v.tensor_scalar(
                out=k.e16[:, c2 * 128:(c2 + 1) * 128],
                in0=k.e16i[:], scalar1=0.0, scalar2=None, op0=Alu.is_equal))

    @blk.gpsimd
    def _(gp):
        gp.wait_ge(pv.h, k.PV_WSUM)
        gp.dma_start(out=k.bo_b[:].unsqueeze(1),
                     in_=k.b_o.partition_broadcast(128)).then_inc(gd.h, 16)
        gd.v += 16
    GD_BO = gd.v

    pa_mod = [0, 0, 0]
    pt_fus = [0, 0, 0]
    pa_sc = [0, 0]
    pv_gqb = [0, 0]
    pv_fev = [0, 0]
    pv_oev = [0, 0]
    PV_ACC = [0] * NT
    pending_wo = []
    pending_fu = []

    def fus_ap(s, c2, lo, hi):
        if s % 2 == 0:
            return k.fus[lo:hi, c2 * SW:(c2 + 1) * SW]
        return k.qT[lo:hi, c2 * T + (s - 1) * SW:c2 * T + s * SW]

    for s in range(NST):
        for c2 in range(NCH):
            jf = c2 % 2

            @blk.tensor
            def _(tn, s=s, c2=c2):
                if s == 0 and c2 == 0:
                    tn.wait_ge(pv.h, PV_P1)
                    tn.wait_ge(pv.h, PV_E16[-1])
                if pv_gqb[0]:
                    tn.wait_ge(pv.h, pv_gqb[0])
                pt.inc(tn.matmul(k.ps_mm[2][:],
                                 k.e16[:, c2 * 128:(c2 + 1) * 128],
                                 k.gq_all[:, s * SW:(s + 1) * SW],
                                 start=True, stop=True))
            PT_GQ = pt.v

            @blk.vector
            def _(v, s=s, c2=c2):
                v.wait_ge(pt.h, PT_GQ)
                pv_gqb[0] = pv.inc(
                    v.tensor_copy(out=k.gqb[jf][:], in_=k.ps_mm[2][:]))

            for kt2 in range(0, NT, 2):
                PT_SCP = [0, 0]

                @blk.tensor
                def _(tn, s=s, c2=c2, kt2=kt2, PT_SCP=PT_SCP):
                    for q in range(2):
                        kt = kt2 + q
                        if pa_sc[q]:
                            tn.wait_ge(pa.h, pa_sc[q])
                        kA = k.kT[0:64,
                                  c2 * T + kt * 128:c2 * T + (kt + 1) * 128]
                        kB = k.kT[64:128,
                                  c2 * T + kt * 128:c2 * T + (kt + 1) * 128]
                        qA = k.qT[0:64, c2 * T + s * SW:c2 * T + (s + 1) * SW]
                        qB = k.qT[64:128,
                                  c2 * T + s * SW:c2 * T + (s + 1) * SW]
                        tn.matmul(k.ps_sc[q][:, 0:SW], kA, qA, start=True,
                                  stop=True, tile_position=(0, 0))
                        PT_SCP[q] = pt.inc(tn.matmul(
                            k.ps_sc[q][:, SW:2 * SW], kB, qB,
                            start=True, stop=True, tile_position=(64, 0)))

                PA_MODP = [0, 0]

                @blk.scalar
                def _(sc, s=s, c2=c2, kt2=kt2, PT_SCP=PT_SCP,
                      PA_MODP=PA_MODP):
                    sc.wait_ge(pt.h, PT_SCP[1])
                    for q in range(2):
                        jm = (kt2 + q) % 3
                        if pt_fus[jm]:
                            sc.wait_ge(pt.h, pt_fus[jm])
                    for q in range(2):
                        jm = (kt2 + q) % 3
                        a = sc.activation(
                            out=k.modb[jm][:], in_=k.ps_sc[q][:],
                            func=Act.Sigmoid, bias=-8.1, scale=30.0)
                        pa_sc[q] = pa.inc(a)
                        PA_MODP[q] = pa_sc[q]

                def emit_fused(s=s, c2=c2, kt2=kt2, PA_MODP=PA_MODP):
                    @blk.tensor
                    def _(tn):
                        for q in range(2):
                            kt = kt2 + q
                            jm = kt % 3
                            tn.wait_ge(pa.h, PA_MODP[q])
                            if kt == 0 and pv_fev[0]:
                                tn.wait_ge(pv.h, pv_fev[0])
                            vA = k.vtm[:, kt * D + c2 * 128:
                                       kt * D + c2 * 128 + 64]
                            vB = k.vtm[:, kt * D + c2 * 128 + 64:
                                       kt * D + (c2 + 1) * 128]
                            tn.matmul(k.ps_mm[0][0:64, :], vA,
                                      k.modb[jm][:, 0:SW],
                                      start=(kt == 0), stop=(kt == NT - 1),
                                      tile_position=(0, 0))
                            pt_fus[jm] = pt.inc(tn.matmul(
                                k.ps_mm[1][64:128, :], vB,
                                k.modb[jm][:, SW:2 * SW],
                                start=(kt == 0), stop=(kt == NT - 1),
                                tile_position=(0, 64)))

                if pending_fu:
                    pending_fu.pop(0)()
                pending_fu.append(emit_fused)

            if pending_fu:
                pending_fu.pop(0)()
            PT_F = pt.v

            @blk.vector
            def _(v, s=s, c2=c2):
                v.wait_ge(pt.h, PT_F)
                v.tensor_tensor(out=fus_ap(s, c2, 0, 64),
                                in0=k.ps_mm[0][0:64, :],
                                in1=k.gqb[jf][0:64, :], op=Alu.mult)
                pv_fev[0] = pv.inc(v.tensor_tensor(
                    out=fus_ap(s, c2, 64, 128),
                    in0=k.ps_mm[1][64:128, :],
                    in1=k.gqb[jf][64:128, :], op=Alu.mult))

            if pending_wo:
                pending_wo.pop(0)()

        PV_FS = pv.v

        def make_wo(s, t4, half, tt_g, PV_FS):
            def emit():
                @blk.tensor
                def _(tn, s=s, t4=t4, half=half):
                    tn.wait_ge(pv.h, PV_FS)
                    if s == 0 and t4 == 0 and half == 0:
                        tn.wait_ge(gd.h, k.GD_WO)
                    if pv_oev[0]:
                        tn.wait_ge(pv.h, pv_oev[0])
                    for c2 in range(NCH):
                        mm = tn.matmul(
                            k.ps_mm[3][:],
                            fus_ap(s, c2, 0, 128)[:, t4 * 128:(t4 + 1) * 128],
                            k.wob[:, c2 * D + half * SW:c2 * D + (half + 1) * SW],
                            start=(c2 == 0), stop=(c2 == NCH - 1))
                    pt.inc(mm)
                PT_O = pt.v

                @blk.vector
                def _(v, s=s, t4=t4, half=half, tt_g=tt_g):
                    v.wait_ge(pt.h, PT_O)
                    if s == 0 and t4 == 0 and half == 0:
                        v.wait_ge(gd.h, GD_BO)
                    pv_oev[0] = pv.inc(v.tensor_tensor(
                        out=k.acc[:, tt_g * D + half * SW:
                                  tt_g * D + (half + 1) * SW],
                        in0=k.ps_mm[3][:],
                        in1=k.bo_b[:, half * SW:(half + 1) * SW],
                        op=Alu.add))
                    PV_ACC[tt_g] = pv_oev[0]
            return emit

        for t4 in range(4):
            for half in range(2):
                pending_wo.append(make_wo(s, t4, half, s * 4 + t4, PV_FS))

    while pending_wo:
        pending_wo.pop(0)()

    k.PV_ACC = PV_ACC
    k.PV_P2 = pv.v
    k.PT_P2 = pt.v


def emit_p3(k):
    nc, blk = k.nc, k.blk
    ld, gd, so, pv, pa, pt, pg = (k.ld, k.gd, k.so, k.pv, k.pa, k.pt, k.pg)
    PT_P2, PV_P2 = k.PT_P2, k.PV_P2

    @blk.vector
    def _(v):
        v.wait_ge(pv.h, k.PV_WSUM)
        rv = k.rowv
        v.tensor_scalar(out=rv[0:1, 0:3], in0=k.wsum[0:1, 0:3],
                        scalar1=1.0 / (D * INNER), scalar2=None, op0=Alu.mult)
        v.drain()
        v.tensor_scalar(out=rv[0:1, 0:3], in0=rv[0:1, 0:3], scalar1=1e-5,
                        scalar2=None, op0=Alu.max)
        v.drain()
        v.tensor_scalar(out=rv[0:1, 3:6], in0=rv[0:1, 0:3], scalar1=0.5,
                        scalar2=None, op0=Alu.mult)
        v.drain()
        v.tensor_scalar(out=rv[0:1, 6:9], in0=rv[0:1, 3:6], scalar1=-1.0,
                        scalar2=None, op0=Alu.mult)
        pv.inc(v.tensor_tensor(out=rv[0:1, 9:10], in0=rv[0:1, 0:1],
                               in1=rv[0:1, 1:2], op=Alu.mult))
    PV_ROW = pv.v

    @blk.tensor
    def _(tn):
        tn.wait_ge(pv.h, PV_ROW)
        pt.inc(tn.matmul(k.ps_mm[2][:, 0:12], k.onesr[:], k.rowv[:],
                         start=True, stop=True))
    PT_ROW = pt.v

    @blk.vector
    def _(v):
        v.wait_ge(pt.h, PT_ROW)
        v.tensor_copy(out=k.frep[:], in_=k.ps_mm[2][:, 0:12])
        v.drain()
        v.tensor_scalar(out=k.cgsx[:], in0=k.sxi[:],
                        scalar1=k.frep[:, 0:1], scalar2=None, op0=Alu.mult)
        v.tensor_tensor(out=k.kfac[:], in0=k.sxi[:], in1=k.sxi[:],
                        op=Alu.mult)
        v.drain()
        v.tensor_scalar(out=k.kfac[:], in0=k.kfac[:],
                        scalar1=k.frep[:, 9:10], scalar2=None, op0=Alu.mult)
        pv.inc(v.tensor_tensor(out=k.rs1sx[:], in0=k.rs1[:], in1=k.sxc[:],
                               op=Alu.mult))
    PV_FC = pv.v

    def wq_dram(mat, c, it):
        d = [k.wgq_d, k.wuq_d, k.wdq_d][mat]
        if mat < 2:
            return d[c * 128:(c + 1) * 128, it * SW:(it + 1) * SW]
        return d[c * 128:(c + 1) * 128, it * SW:(it + 1) * SW]

    def wf_dram(mat, c, it):
        d = [k.wgT, k.wuT, k.wdT][mat]
        return d[c * 128:(c + 1) * 128, it * SW:(it + 1) * SW]

    pt_slot = [0] * 16
    pv_wfq = [0, 0, 0, 0]
    gd_wq = [0] * 16
    PV_XQD = [0, 0]
    PA_XREL = [0, 0]
    LD_XT = [0, 0]
    PA_XQI = [0, 0]
    pv_pb = [0, 0, 0, 0]
    pv_pd = [0, 0, 0, 0]
    pt_tf = [0, 0]
    LD_X = [0, 0]
    LD_WQ = [0] * 16
    PV_OUT = [0, 0]
    SO_OUT = [0, 0]

    def load_wq_tile(mat, it, slot_base, g):
        for c in range(8):
            slot = slot_base + c
            cc = c if mat < 2 else it * 16 // SW
            if g == 0:
                i2 = (mat * 64 + it * 8 + c) % 4

                @blk.sync
                def _(sy, mat=mat, c=c, it=it, i2=i2, slot=slot):
                    if pv_wfq[i2]:
                        sy.wait_ge(pv.h, pv_wfq[i2])
                    else:
                        sy.wait_ge(pa.h, max(PA_XREL))
                    sy.dma_start(out=k.wfq[i2][:],
                                 in_=wf_dram(mat, c, it)).then_inc(ld.h, 16)
                    ld.v += 16
                LDF = ld.v

                @blk.vector
                def _(v, mat=mat, i2=i2, slot=slot):
                    v.wait_ge(ld.h, LDF)
                    if pt_slot[slot]:
                        v.wait_ge(pt.h, pt_slot[slot])
                    else:
                        v.wait_ge(pv.h, PV_P2)
                        v.wait_ge(pt.h, PT_P2)
                    thr = k.frep[:, 3 + mat:4 + mat]
                    nthr = k.frep[:, 6 + mat:7 + mat]
                    v.tensor_scalar(out=k.qa16[0][:], in0=k.wfq[i2][:],
                                    scalar1=thr, scalar2=None, op0=Alu.is_ge)
                    pv_wfq[i2] = pv.inc(v.tensor_scalar(
                        out=k.qa16[1][:], in0=k.wfq[i2][:],
                        scalar1=nthr, scalar2=None, op0=Alu.is_le))
                    v.wait_ge(pv.h, pv_wfq[i2])
                    pv.inc(v.tensor_tensor(out=k.wqt[slot][:],
                                           in0=k.qa16[0][:],
                                           in1=k.qa16[1][:],
                                           op=Alu.subtract))
                PVQ = pv.v

                @blk.gpsimd
                def _(gp, mat=mat, c=c, it=it, slot=slot):
                    gp.wait_ge(pv.h, PVQ)
                    gp.dma_start(out=wq_dram(mat, c, it),
                                 in_=k.wqt[slot][:]).then_inc(gd.h, 16)
                    gd.v += 16
                k._wq_written[(mat, c, it)] = gd.v
                k._wq_ready[slot] = ("pv", PVQ)
            else:

                @blk.gpsimd
                def _(gp, mat=mat, c=c, it=it, slot=slot):
                    if pt_slot[slot]:
                        gp.wait_ge(pt.h, pt_slot[slot])
                    gp.wait_ge(gd.h, k._wq_written[(mat, c, it)])
                    gp.dma_start(out=k.wqt[slot][:],
                                 in_=wq_dram(mat, c, it)).then_inc(gd.h, 16)
                    gd.v += 16
                k._wq_ready[slot] = ("gd", gd.v)

    k._wq_ready = [None] * 16
    k._wq_written = {}

    for g in range(NG):
        for tt in range(4):
            t = g * 4 + tt
            j = tt % 2

            @blk.sync
            def _(sy, t=t, j=j, g=g):
                if g == 0 and t < 2:
                    sy.wait_ge(pt.h, PT_P2)
                    sy.wait_ge(pv.h, PV_P2)
                elif SO_OUT[j]:
                    sy.wait_ge(so.h, SO_OUT[j])
                if PA_XREL[j]:
                    sy.wait_ge(pa.h, PA_XREL[j])
                sy.dma_start(out=k.xrel[j][:],
                             in_=k.xd[t * 128:(t + 1) * 128, :]).then_inc(
                                 ld.h, 16)
                ld.v += 16
            LD_X[j] = ld.v

            @blk.vector
            def _(v, t=t, j=j, g=g):
                v.wait_ge(ld.h, LD_X[j])
                if g == 0 and t < 2:
                    v.wait_ge(pv.h, PV_FC)
                v.tensor_scalar(out=k.xrel[j][:], in0=k.xrel[j][:],
                                scalar1=k.mv1[:, 2 * t:2 * t + 1],
                                scalar2=k.rs1sx[:, t:t + 1],
                                op0=Alu.subtract, op1=Alu.mult)
                v.drain()
                v.tensor_scalar(out=k.xrel[j][:], in0=k.xrel[j][:],
                                scalar1=MAGIC, scalar2=-MAGIC,
                                op0=Alu.add, op1=Alu.add)
                v.drain()
                pv.inc(v.tensor_scalar(out=k.xrel[j][:], in0=k.xrel[j][:],
                                       scalar1=128.25, scalar2=None,
                                       op0=Alu.add))
            PV_XQ = pv.v

            jt0, jt1 = (2 * tt) % 2, (2 * tt + 1) % 2

            @blk.scalar
            def _(sc, j=j, jt0=jt0, jt1=jt1):
                sc.wait_ge(pv.h, PV_XQ)
                for jt in (jt0, jt1):
                    if PV_XQD[jt]:
                        sc.wait_ge(pv.h, PV_XQD[jt])
                for cc, jt in ((0, jt0), (1, jt1)):
                    a = sc.activation(
                        out=k.xqi[jt][:],
                        in_=k.xrel[j][:, cc * 512:(cc + 1) * 512],
                        func=Act.Copy)
                    PA_XQI[jt] = pa.inc(a)
            for cc, jt in ((0, jt0), (1, jt1)):

                @blk.vector
                def _(v, jt=jt, cc=cc, tt=tt, g=g):
                    v.wait_ge(pa.h, PA_XQI[jt])
                    if g == 0 and tt == 0 and cc == 0:
                        v.wait_ge(pt.h, PT_P2)
                    if LD_XT[jt]:
                        v.wait_ge(ld.h, LD_XT[jt])
                    PV_XQD[jt] = pv.inc(v.tensor_scalar(
                        out=k.xq16[jt][:], in0=k.xqi[jt][:],
                        scalar1=-128.0, scalar2=None, op0=Alu.add))
                PV_XQ16 = pv.v

                @blk.sync
                def _(sy, jt=jt, cc=cc, tt=tt, PV_XQ16=PV_XQ16):
                    sy.wait_ge(pv.h, PV_XQ16)
                    for b in range(4):
                        c = cc * 4 + b
                        sy.dma_start_transpose(
                            k.xqTg[:, c * GW + tt * 128:c * GW + (tt + 1) * 128],
                            k.xq16[jt][:, b * 128:(b + 1) * 128]).then_inc(
                                ld.h, 16)
                    ld.v += 4 * 16
                    LD_XT[jt] = ld.v
            PA_XREL[j] = pa.v

        LD_XQ_DONE = ld.v

        for mat in range(2):
            for it in range(NIT):
                sb = (it % 2) * 8
                load_wq_tile(mat, it, sb, g)

                def gu_mm(tt, jp, mat=mat, it=it, sb=sb, g=g):
                    @blk.tensor
                    def _(tn, mat=mat, it=it, tt=tt, jp=jp, sb=sb):
                        if mat == 0 and it == 0 and tt == 0:
                            tn.wait_ge(ld.h, LD_XQ_DONE)
                        wmax = {}
                        for c in range(8):
                            rd = k._wq_ready[sb + c]
                            if rd is not None:
                                wmax[rd[0]] = max(wmax.get(rd[0], 0), rd[1])
                                k._wq_ready[sb + c] = None
                        for typ, val in wmax.items():
                            tn.wait_ge({"pv": pv, "ld": ld,
                                        "gd": gd}[typ].h, val)
                        for c in range(8):
                            if c == 0 and pv_pb[jp]:
                                tn.wait_ge(pv.h, pv_pb[jp])
                            if c == 0 and pv_pd[jp]:
                                tn.wait_ge(pv.h, pv_pd[jp])
                            mm = tn.matmul(
                                k.ps_mm[jp][:],
                                k.xqTg[:, c * GW + tt * 128:
                                       c * GW + (tt + 1) * 128],
                                k.wqt[sb + c][:],
                                start=(c == 0), stop=(c == 7))
                        pt.inc(mm)
                        for c in range(8):
                            pt_slot[sb + c] = pt.v
                    return pt.v

                if mat == 0:
                    for tth in range(2):
                        PT_P = [0, 0]
                        for q in range(2):
                            tt = 2 * tth + q
                            PT_P[q] = gu_mm(tt, tt)
                        PA_P = [0, 0]

                        @blk.scalar
                        def _(sc, g=g, it=it, tth=tth, PT_P=PT_P,
                              PA_P=PA_P):
                            sc.wait_ge(pt.h, PT_P[1])
                            for q in range(2):
                                js = (2 * tth + q) % 2
                                if k._pa_sg[js]:
                                    sc.wait_ge(pv.h, k._pa_sg[js])
                            for q in range(2):
                                tt = 2 * tth + q
                                t = g * 4 + tt
                                PA_P[q] = pa.inc(sc.activation(
                                    out=k.sgb[tt % 2][:],
                                    in_=k.ps_mm[tt][:],
                                    func=Act.Sigmoid,
                                    scale=k.cgsx[:, t:t + 1]))

                        for q in range(2):
                            tt = 2 * tth + q

                            @blk.vector
                            def _(v, it=it, tt=tt, q=q, PA_P=PA_P):
                                v.wait_ge(pa.h, PA_P[q])
                                pv_pb[tt] = pv.inc(v.scalar_tensor_tensor(
                                    out=k.pbuf[:, tt * INNER + it * SW:
                                               tt * INNER + (it + 1) * SW],
                                    in0=k.ps_mm[tt][:], scalar=1.0,
                                    in1=k.sgb[tt % 2][:], op0=Alu.mult,
                                    op1=Alu.mult))
                                k._pa_sg[tt % 2] = pv_pb[tt]
                else:
                    for tt in range(4):
                        jp = tt
                        PT_GU = gu_mm(tt, jp)

                        @blk.vector
                        def _(v, it=it, tt=tt, jp=jp, PT_GU=PT_GU):
                            v.wait_ge(pt.h, PT_GU)
                            sl = k.pbuf[:, tt * INNER + it * SW:
                                        tt * INNER + (it + 1) * SW]
                            pv_pb[jp] = pv.inc(v.scalar_tensor_tensor(
                                out=sl, in0=k.ps_mm[jp][:], scalar=1.0,
                                in1=sl, op0=Alu.mult, op1=Alu.mult))
        PV_GU = pv.v

        LDH = [[0, 0] for _ in range(4)]
        for tt in range(4):
            t = g * 4 + tt

            @blk.vector
            def _(v, t=t, tt=tt):
                if tt == 0:
                    v.wait_ge(pv.h, PV_GU)
                sl = k.pbuf[:, tt * INNER:(tt + 1) * INNER]
                v.reduce_max(out=k.scr[:, 11:12], in_=sl, axis=AX,
                             apply_absolute_value=True)
                v.drain()
                v.tensor_scalar(out=k.scr[:, 11:12], in0=k.scr[:, 11:12],
                                scalar1=k.kfac[:, t:t + 1], scalar2=None,
                                op0=Alu.mult)
                v.drain()
                v.tensor_scalar(out=k.scr[:, 11:12], in0=k.scr[:, 11:12],
                                scalar1=1e-5, scalar2=None, op0=Alu.max)
                v.drain()
                v.reciprocal(out=k.scr[:, 12:13], in_=k.scr[:, 11:12])
                v.drain()
                v.tensor_scalar(out=k.scr[:, 12:13], in0=k.scr[:, 12:13],
                                scalar1=127.0,
                                scalar2=k.kfac[:, t:t + 1],
                                op0=Alu.mult, op1=Alu.mult)
                v.drain()
                v.tensor_scalar(out=k.cdsh[:, tt:tt + 1],
                                in0=k.scr[:, 11:12],
                                scalar1=k.frep[:, 2:3],
                                scalar2=1.0 / 127.0,
                                op0=Alu.mult, op1=Alu.mult)
                v.tensor_scalar(out=sl, in0=sl,
                                scalar1=k.scr[:, 12:13], scalar2=MAGIC,
                                op0=Alu.mult, op1=Alu.add)
                v.drain()
                pv.inc(v.tensor_scalar(out=sl, in0=sl, scalar1=-MAGIC,
                                       scalar2=128.25, op0=Alu.add,
                                       op1=Alu.add))
            PV_H = pv.v

            for sc8 in range(0, 8, 2):
                @blk.scalar
                def _(s_, tt=tt, sc8=sc8):
                    s_.wait_ge(pv.h, PV_H)
                    for q in range(2):
                        jt = (8 * tt + sc8 + q) % 2
                        if PV_XQD[jt]:
                            s_.wait_ge(pv.h, PV_XQD[jt])
                    for q in range(2):
                        jt = (8 * tt + sc8 + q) % 2
                        sl = k.pbuf[:, tt * INNER + (sc8 + q) * SW:
                                    tt * INNER + (sc8 + q + 1) * SW]
                        a = s_.activation(out=k.xqi[jt][:], in_=sl,
                                          func=Act.Copy)
                        PA_XQI[jt] = pa.inc(a)

                for q in range(2):
                    jt = (8 * tt + sc8 + q) % 2

                    @blk.vector
                    def _(v, jt=jt):
                        v.wait_ge(pa.h, PA_XQI[jt])
                        if LD_XT[jt]:
                            v.wait_ge(ld.h, LD_XT[jt])
                        PV_XQD[jt] = pv.inc(v.tensor_scalar(
                            out=k.xq16[jt][:], in0=k.xqi[jt][:],
                            scalar1=-128.0, scalar2=None, op0=Alu.add))
                    PV_HB = pv.v

                    @blk.sync
                    def _(sy, tt=tt, sc8=sc8, q=q, jt=jt, PV_HB=PV_HB):
                        sy.wait_ge(pv.h, PV_HB)
                        for b in range(4):
                            ch = (sc8 + q) * 4 + b
                            sy.dma_start_transpose(
                                k.hT[:, ch * GW + tt * 128:
                                     ch * GW + (tt + 1) * 128],
                                k.xq16[jt][:, b * 128:(b + 1) * 128]
                            ).then_inc(ld.h, 16)
                        ld.v += 4 * 16
                        LD_XT[jt] = ld.v
                LDH[tt][sc8 // 4] = ld.v
        LD_H_DONE = ld.v
        PG_H = pg.v

        for ot in range(2):
            for chh in range(2):
                for c16 in range(16):
                    ch = chh * 16 + c16
                    slot = c16
                    it_eq = ot
                    if g == 0:
                        i2 = (ch + ot * 32) % 4

                        @blk.sync
                        def _(sy, ch=ch, ot=ot, i2=i2):
                            if pv_wfq[i2]:
                                sy.wait_ge(pv.h, pv_wfq[i2])
                            sy.dma_start(out=k.wfq[i2][:],
                                         in_=wf_dram(2, ch, ot)).then_inc(
                                             ld.h, 16)
                            ld.v += 16
                        LDF = ld.v

                        @blk.vector
                        def _(v, ch=ch, ot=ot, i2=i2, slot=slot):
                            v.wait_ge(ld.h, LDF)
                            if pt_slot[slot]:
                                v.wait_ge(pt.h, pt_slot[slot])
                            v.tensor_scalar(out=k.qa16[0][:],
                                            in0=k.wfq[i2][:],
                                            scalar1=k.frep[:, 5:6],
                                            scalar2=None, op0=Alu.is_ge)
                            pv_wfq[i2] = pv.inc(v.tensor_scalar(
                                out=k.qa16[1][:], in0=k.wfq[i2][:],
                                scalar1=k.frep[:, 8:9],
                                scalar2=None, op0=Alu.is_le))
                            v.wait_ge(pv.h, pv_wfq[i2])
                            pv.inc(v.tensor_tensor(out=k.wqt[slot][:],
                                                   in0=k.qa16[0][:],
                                                   in1=k.qa16[1][:],
                                                   op=Alu.subtract))
                        PVQ = pv.v

                        @blk.gpsimd
                        def _(gp, ch=ch, ot=ot, slot=slot):
                            gp.wait_ge(pv.h, PVQ)
                            gp.dma_start(out=wq_dram(2, ch, ot),
                                         in_=k.wqt[slot][:]).then_inc(gd.h, 16)
                            gd.v += 16
                        k._wq_written[(2, ch, ot)] = gd.v
                        k._wq_ready[slot] = ("pv", PVQ)
                    else:

                        @blk.gpsimd
                        def _(gp, ch=ch, ot=ot, slot=slot):
                            if pt_slot[slot]:
                                gp.wait_ge(pt.h, pt_slot[slot])
                            gp.wait_ge(gd.h, k._wq_written[(2, ch, ot)])
                            gp.dma_start(out=k.wqt[slot][:],
                                         in_=wq_dram(2, ch, ot)).then_inc(
                                             gd.h, 16)
                            gd.v += 16
                        k._wq_ready[slot] = ("gd", gd.v)

                for tt in range(4):

                    @blk.tensor
                    def _(tn, g=g, ot=ot, chh=chh, tt=tt):
                        if ot == 0:
                            tn.wait_ge(ld.h, LDH[tt][chh])
                        if tt == 0:
                            wmax = {}
                            for c16 in range(16):
                                rd = k._wq_ready[c16]
                                if rd is not None:
                                    wmax[rd[0]] = max(wmax.get(rd[0], 0),
                                                      rd[1])
                            for typ, val in wmax.items():
                                tn.wait_ge({"pv": pv, "ld": ld, "gd": gd}[typ].h, val)
                        for c16 in range(16):
                            ch = chh * 16 + c16
                            if c16 == 0 and chh == 0 and pv_pd[tt]:
                                tn.wait_ge(pv.h, pv_pd[tt])
                            mm = tn.matmul(
                                k.ps_mm[tt][:],
                                k.hT[:, ch * GW + tt * 128:
                                     ch * GW + (tt + 1) * 128],
                                k.wqt[c16][:],
                                start=(chh == 0 and c16 == 0),
                                stop=(chh == 1 and c16 == 15))
                        if tt == 3:
                            for c16 in range(16):
                                k._wq_ready[c16] = None
                        pt.inc(mm)
                        for c16 in range(16):
                            pt_slot[c16] = pt.v
                    PT_D = pt.v

                    if chh == 1:

                        @blk.vector
                        def _(v, g=g, ot=ot, tt=tt):
                            t = g * 4 + tt
                            v.wait_ge(pt.h, PT_D)
                            asl = k.acc[:, t * D + ot * SW:
                                        t * D + (ot + 1) * SW]
                            pv_pd[tt] = pv.inc(v.scalar_tensor_tensor(
                                out=asl, in0=k.ps_mm[tt][:],
                                scalar=k.cdsh[:, tt:tt + 1], in1=asl,
                                op0=Alu.mult, op1=Alu.add))
        PV_D = pv.v

        for tt in range(4):
            t = g * 4 + tt
            j = tt % 2

            @blk.sync
            def _(sy, t=t, j=j, tt=tt):
                sy.wait_ge(pv.h, PV_D)
                if SO_OUT[j]:
                    sy.wait_ge(so.h, SO_OUT[j])
                sy.dma_start(out=k.xrel[j][:],
                             in_=k.xd[t * 128:(t + 1) * 128, :]).then_inc(
                                 ld.h, 16)
                ld.v += 16
            LD_XF = ld.v

            @blk.vector
            def _(v, t=t, j=j):
                v.wait_ge(ld.h, LD_XF)
                pv.inc(v.tensor_tensor(out=k.xrel[j][:], in0=k.xrel[j][:],
                                       in1=k.acc[:, t * D:(t + 1) * D],
                                       op=Alu.add))
            PV_O = pv.v
            PV_OUT[j] = PV_O

            @blk.sync
            def _(sy, t=t, j=j):
                sy.wait_ge(pv.h, PV_O)
                sy.dma_start(out=k.yd[t * 128:(t + 1) * 128, :],
                             in_=k.xrel[j][:]).then_inc(so.h, 16)
                so.v += 16
            SO_OUT[j] = so.v

    @blk.sync
    def _(sy):
        sy.wait_ge(so.h, so.v)
        sy.wait_ge(gd.h, gd.v)


def emit_taps(k):
    nc, blk = k.nc, k.blk
    so, pv, pa, pt, pg = k.so, k.pv, k.pa, k.pt, k.pg
    src_map = lambda: {
        "xnT": k.xnT, "xmT": k.xmT, "mv1": k.mv1,
        "rs1": k.rs1, "sxc": k.sxc, "sxi": k.sxi,
        "kT": k.kT, "qT": k.qT, "vtm": k.vtm, "acc": k.acc,
        "gq_all": k.gq_all, "gk_all": k.gk_all, "gk_tok": k.gk_tok,
        "idnf": k.idnf, "e2f": k.e2f,
        "fus": k.fus, "wsum": k.wsum, "sxi": k.sxi,
    }

    @blk.sync
    def _(sy):
        sy.wait_ge(pv.h, pv.v)
        sy.wait_ge(pa.h, pa.v)
        sy.wait_ge(pt.h, pt.v)
        sy.wait_ge(pg.h, pg.v)
        m = src_map()
        for name, ap in k.taps.items():
            sy.dma_start(out=ap, in_=m[name][:]).then_inc(so.h, 16)
            so.v += 16
        sy.wait_ge(so.h, so.v)


def kernel(**inputs):
    xs = np.ascontiguousarray(np.asarray(inputs["L0_attn_x"], np.float32))
    B = xs.shape[0]
    base = make_base_inputs(inputs)
    nc, _ = build_program()
    in_maps = [dict(base, x=np.ascontiguousarray(xs[c])) for c in range(B)]
    res = run_bass_kernel_spmd(nc, in_maps, core_ids=list(range(B)))
    return np.stack([res.results[c]["y"] for c in range(B)], axis=0)


def make_base_inputs(inputs):
    return {
        "w_q": np.ascontiguousarray(inputs["L0_attn_w_q"], dtype=np.float32),
        "w_k": np.ascontiguousarray(inputs["L0_attn_w_k"], dtype=np.float32),
        "w_v": np.ascontiguousarray(inputs["L0_attn_w_v"], dtype=np.float32),
        "w_o": np.ascontiguousarray(inputs["L0_attn_w_o"], dtype=np.float32),
        "g_q": np.ascontiguousarray(
            np.asarray(inputs["L0_attn_g_q"], np.float32).reshape(HD, 1)),
        "g_k": np.ascontiguousarray(
            np.asarray(inputs["L0_attn_g_k"], np.float32).reshape(HD, 1)),
        "b_o": np.ascontiguousarray(
            np.asarray(inputs["L0_attn_b_o"], np.float32).reshape(1, D)),
        "gamma": np.ascontiguousarray(
            np.asarray(inputs["L0_attn_gamma"], np.float32).reshape(1, D)),
        "beta": np.ascontiguousarray(
            np.asarray(inputs["L0_attn_beta"], np.float32).reshape(1, D)),
        "wgT": np.ascontiguousarray(
            np.asarray(inputs["L0_ffn_w_gate"], np.float32).T),
        "wuT": np.ascontiguousarray(
            np.asarray(inputs["L0_ffn_w_up"], np.float32).T),
        "wdT": np.ascontiguousarray(
            np.asarray(inputs["L0_ffn_w_down"], np.float32).T),
    }


if __name__ == "__main__":
    print("kernel module ok")
